# version 02 (final)
# baseline (speedup 1.0000x reference)
import numpy as np
import ml_dtypes
import concourse.bass as bass
import concourse.mybir as mybir
from concourse.bass_utils import run_bass_kernel_spmd

AF = mybir.ActivationFunctionType
ALU = mybir.AluOpType
AX = mybir.AxisListType
F32 = mybir.dt.float32
BF16 = mybir.dt.bfloat16
BFNP = ml_dtypes.bfloat16

DIM = 1024
INNER = 2048
HEADS = 8
STATE = 64
HD = 256
KER = 4
BC = 2
S = 2048
TILE = 512
NT = S // TILE
NSUB = TILE // 128
NCH = TILE // 128
KD = DIM // 128
KI = INNER // 128
FF = 4096
NE = 8
KF = FF // 128
EPS = 1e-5
NORM_GN = float(S * HD)

SB_BASE = 16512
SB_TOP = 229344
CW = KER - 1 + TILE


class Sem:
    __slots__ = ("h", "n")

    def __init__(self, h):
        self.h = h
        self.n = 0


class PB:
    ENGS = ("tensor", "vector", "scalar", "sync", "gpsimd")

    CHAIN = ("vector", "scalar", "gpsimd")

    def __init__(self, nc, strict=False):
        self.nc = nc
        self.strict = strict
        self.steps = {e: [] for e in self.ENGS}
        self.last = {e: None for e in self.ENGS}

    def emit(self, eng, fn, wait=(), inc=None, inc_n=1):
        w = tuple(t for t in wait if t is not None)
        if (self.strict or eng in self.CHAIN) and self.last[eng] is not None:
            w = w + (self.last[eng],)
        tok = None
        if inc is not None:
            inc.n += inc_n
            tok = (inc, inc.n)
            self.last[eng] = tok
        self.steps[eng].append((w, fn, (inc, inc_n) if inc is not None else None))
        return tok

    def replay(self, block):
        def mk(ename):
            def body(eng):
                for w, fn, inc in self.steps[ename]:
                    for s, v in w:
                        eng.wait_ge(s.h, v)
                    inst = fn(eng) if fn is not None else None
                    if inc is not None:
                        s, n = inc
                        if inst is None:
                            inst = eng.nop()
                        inst.then_inc(s.h, n)
            return body
        block.tensor(mk("tensor"))
        block.vector(mk("vector"))
        block.scalar(mk("scalar"))
        block.sync(mk("sync"))
        block.gpsimd(mk("gpsimd"))


LAST_MAP = {}


class SbufMap:
    def __init__(self, nc):
        self.nc = nc
        self.cur = SB_BASE
        self.marks = []
        self.hi = SB_BASE
        LAST_MAP.clear()

    def alloc(self, name, shape, dtype, align=128):
        per_part = int(np.prod(shape[1:])) * (4 if dtype == F32 else 2)
        self.cur = (self.cur + align - 1) // align * align
        assert self.cur + per_part <= SB_TOP, \
            f"SBUF overflow at {name}: {self.cur}+{per_part} > {SB_TOP}"
        h = self.nc.alloc_sbuf_tensor_at(name, list(shape), dtype, offset=self.cur)
        LAST_MAP[name] = (self.cur, tuple(shape), dtype)
        self.cur += per_part
        self.hi = max(self.hi, self.cur)
        return h

    def mark(self):
        self.marks.append(self.cur)

    def reset(self):
        self.cur = self.marks[-1]


def build(debug_taps=(), strict=False, scan_mask=7, barriers=0, delay_w=0, warmup=0, dup_w=0,
          nb=BC, nt=NT, stop_after=None, ne3=NE, mmv=7):
    nc = bass.Bass("TRN2", target_bir_lowering=False,
                   detect_race_conditions=bool(strict))
    DT = set(debug_taps)

    def inp(name, shape, dt=F32):
        return nc.dram_tensor(name, list(shape), dt, kind="ExternalInput").ap()

    x_in = inp("x", [BC, S, DIM])
    Wx = inp("Wx", [DIM, INNER])
    Wz = inp("Wz", [DIM, INNER])
    convw = inp("convw", [128, KI * KER])
    convb = inp("convb", [128, KI])
    Wb = inp("Wb", [INNER, HEADS * STATE])
    Wc = inp("Wc", [INNER, HEADS * STATE])
    Wdt = inp("Wdt", [INNER, HEADS])
    bdt = inp("bdt", [HEADS, 1])
    loga = inp("loga", [HEADS, 1])
    dpar = inp("dpar", [HEADS, 1])
    gnw = inp("gnw", [128, KI])
    gnb = inp("gnb", [128, KI])
    Wout = inp("Wout", [INNER, DIM])
    w1 = inp("w1", [NE, DIM, FF])
    w2 = inp("w2", [NE, FF, DIM])
    b2t_in = inp("b2t_in", [128, KD * NE])
    eye_bf = inp("eye_bf", [128, 128], BF16)
    eye_f = inp("eye_f", [128, 128])
    triu_f = inp("triu_f", [128, 128])
    ones_col_bf = inp("ones_col_bf", [128, 1], BF16)
    ones_row_f = inp("ones_row_f", [1, 128])
    selh_f = inp("selh_f", [HEADS, HEADS * 128])
    selc_f = inp("selc_f", [HEADS, 4 * 128])
    pair16_f = inp("pair16_f", [16, HEADS])
    dblrow_f = inp("dblrow_f", [HEADS, 16])
    consts_in = inp("consts_in", [128, 10])
    b1_in = inp("b1_in", [128, NE * KF])
    gate_in = inp("gate_in", [128, KD * NE])
    zf_in = inp("zf_in", [128, BC * 4 * HD])
    zb_in = inp("zb_in", [128, BC * 4 * HD], BF16)

    out_t = nc.dram_tensor("out", [BC, S, DIM], F32, kind="ExternalOutput").ap()

    def scratch(name, shape, dt):
        return nc.dram_tensor(name, list(shape), dt, kind="Internal").ap()

    y_sp = scratch("y_sp", [BC, KI, 128, S], BF16)
    sz_sp = scratch("sz_sp", [BC, KI, 128, S], BF16)
    x2_sp = scratch("x2_sp", [BC, KD, 128, S], F32)
    ln2_sp = scratch("ln2_sp", [BC, KD, 128, S], BF16)
    acc_sp = scratch("acc_sp", [BC, KD, 128, S], F32)

    taps = {}
    if "xc" in DT:
        taps["xc"] = nc.dram_tensor("tap_xc", [BC, KI, 128, S], BF16, kind="ExternalOutput").ap()
    if "bcdt" in DT:
        taps["bmat"] = nc.dram_tensor("tap_bmat", [BC, 4, 128, S], BF16, kind="ExternalOutput").ap()
        taps["dt"] = nc.dram_tensor("tap_dt", [BC, HEADS, S], F32, kind="ExternalOutput").ap()
    if "y" in DT:
        taps["y"] = nc.dram_tensor("tap_y", [BC, KI, 128, S], BF16, kind="ExternalOutput").ap()
    if "lnf" in DT:
        taps["lnf"] = nc.dram_tensor("tap_lnf", [BC, KD, 128, S], BF16, kind="ExternalOutput").ap()
    if "xsub" in DT:
        taps["xsub"] = nc.dram_tensor("tap_xsub", [NSUB, 128, DIM], F32, kind="ExternalOutput").ap()
        taps["stat"] = nc.dram_tensor("tap_stat", [NSUB, 128, 12], F32, kind="ExternalOutput").ap()
    if "gn" in DT:
        taps["gn"] = nc.dram_tensor("tap_gn", [128, 2 * BC * KI], F32, kind="ExternalOutput").ap()

    sm = SbufMap(nc)
    c_eye_bf = sm.alloc("c_eye_bf", [128, 128], BF16)
    c_eye_f = sm.alloc("c_eye_f", [128, 128], F32)
    c_triu = sm.alloc("c_triu", [128, 128], F32)
    c_ones_col_bf = sm.alloc("c_ones_col_bf", [128, 1], BF16)
    c_ones_row_f = sm.alloc("c_ones_row_f", [1, 128], F32)
    c_selh = sm.alloc("c_selh", [HEADS, HEADS * 128], F32)
    c_selc = sm.alloc("c_selc", [HEADS, 4 * 128], F32)
    c_pair16 = sm.alloc("c_pair16", [16, HEADS], F32)
    c_dblrow = sm.alloc("c_dblrow", [HEADS, 16], F32)
    c_consts = sm.alloc("c_consts", [128, 10], F32)
    convw_cols = sm.alloc("convw_cols", [128, KI * KER], F32)
    convb_col = sm.alloc("convb_col", [128, KI], F32)
    gnw_col = sm.alloc("gnw_col", [128, KI], F32)
    gnb_col = sm.alloc("gnb_col", [128, KI], F32)
    bdt_col = sm.alloc("bdt_col", [HEADS, 1], F32)
    loga_col = sm.alloc("loga_col", [HEADS, 1], F32)
    a_col = sm.alloc("a_col", [HEADS, 1], F32)
    dpar_col = sm.alloc("dpar_col", [HEADS, 1], F32)
    dT = sm.alloc("dT", [128, HEADS], F32)
    junk8 = sm.alloc("junk8", [HEADS, 128], F32)
    zeros8 = sm.alloc("zeros8", [HEADS, 128], F32)
    stats_sum = sm.alloc("stats_sum", [128, BC * KI], F32)
    stats_sq = sm.alloc("stats_sq", [128, BC * KI], F32)
    gn_scale = sm.alloc("gn_scale", [128, BC * KI], F32)
    gn_bias = sm.alloc("gn_bias", [128, BC * KI], F32)
    racc = sm.alloc("racc", [128, BC * KD], F32)
    comb_cols = sm.alloc("comb_cols", [128, BC * NE], F32)
    b2c_cols = sm.alloc("b2c_cols", [128, BC * KD], F32)
    rtmp = sm.alloc("rtmp", [128, 64], F32)
    rtmp8 = sm.alloc("rtmp8", [HEADS, 160], F32)
    b1_cols = sm.alloc("b1_cols", [128, NE * KF], F32)
    gate_sb = sm.alloc("gate_sb", [128, KD * NE], F32)
    b2T = sm.alloc("b2T", [128, KD * NE], F32)

    sm.mark()
    wxT = [sm.alloc(f"wxT{k}", [128, INNER], BF16) for k in range(KD)]
    wzT = [sm.alloc(f"wzT{k}", [128, INNER], BF16) for k in range(KD)]
    wbT = [sm.alloc(f"wbT{k}", [128, HEADS * STATE], BF16) for k in range(KI)]
    wcT = [sm.alloc(f"wcT{k}", [128, HEADS * STATE], BF16) for k in range(KI)]
    wdtT = [sm.alloc(f"wdtT{k}", [128, HEADS], BF16) for k in range(KI)]
    st_f32 = sm.alloc("st_f32", [128, BC * 4 * HD], F32)
    st_bf = sm.alloc("st_bf", [128, BC * 4 * HD], BF16)
    x_tm = sm.alloc("x_tm", [128, 2 * DIM], F32)
    ln1_sub = sm.alloc("ln1_sub", [128, 2 * DIM], BF16)
    ln1_fm = sm.alloc("ln1_fm", [128, KD * TILE], BF16)
    ln1_cols = sm.alloc("ln1_cols", [128, 2 * NSUB], F32)
    xpc = sm.alloc("xpc", [128, 2 * CW], BF16)
    xtail = sm.alloc("xtail", [128, KI * (KER - 1)], BF16)
    xc_bf = sm.alloc("xc_bf", [128, KI * TILE], BF16)
    cvt = sm.alloc("cvt", [128, 2 * TILE], F32)
    sz_bf = sm.alloc("sz_bf", [128, 4 * TILE], BF16)
    bmat = sm.alloc("bmat", [128, 4 * TILE], BF16)
    cmat = sm.alloc("cmat", [128, 4 * TILE], BF16)
    ctil = sm.alloc("ctil", [128, 4 * TILE], BF16)
    dt_fm = sm.alloc("dt_fm", [HEADS, TILE], F32)
    dt_sc = sm.alloc("dt_sc", [HEADS, 3 * TILE], F32)
    alpha = sm.alloc("alpha", [HEADS, TILE], F32)
    alpham1 = sm.alloc("alpham1", [HEADS, 128], F32)
    erow = sm.alloc("erow", [HEADS, 128], F32)
    wexp8 = sm.alloc("wexp8", [HEADS, 128], F32)
    decb8 = sm.alloc("decb8", [HEADS, 128], F32)
    scn_sm = sm.alloc("scn_sm", [128, 2 * 32], F32)
    mt_bf = sm.alloc("mt_bf", [128, 2 * 128], BF16)
    e_f32 = sm.alloc("e_f32", [128, 2 * 128], F32)
    xtm_bf = sm.alloc("xtm_bf", [128, 2 * HD], BF16)
    bw_bf = sm.alloc("bw_bf", [128, 2 * STATE], BF16)
    y_sb = sm.alloc("y_sb", [128, KI * 128], BF16)
    sq_bf = sm.alloc("sq_bf", [128, KI * 128], BF16)
    sm.reset()
    woutT = sm.alloc("woutT", [128, KI * DIM], BF16)
    y_ld = sm.alloc("y_ld", [128, 2 * 4 * TILE], BF16)
    sz_ld = sm.alloc("sz_ld", [128, 2 * 4 * TILE], BF16)
    x_tm2 = sm.alloc("x_tm2", [128, 2 * DIM], F32)
    x_fm = sm.alloc("x_fm", [128, KD * TILE], F32)
    u_bf = sm.alloc("u_bf", [128, KI * TILE], BF16)
    gnt_bf = sm.alloc("gnt_bf", [128, TILE], BF16)
    attn_f = sm.alloc("attn_f", [128, KD * TILE], F32)
    attn_bf = sm.alloc("attn_bf", [128, KD * TILE], BF16)
    sq2_bf = sm.alloc("sq2_bf", [128, KD * TILE], BF16)
    x2_f = sm.alloc("x2_f", [128, KD * TILE], F32)
    ln2_bf = sm.alloc("ln2_bf", [128, KD * TILE], BF16)
    rows_sb = sm.alloc("rows_sb", [1, 4 * TILE], F32)
    r8_sb = sm.alloc("r8_sb", [1, 64], F32)
    sm.reset()
    w1_bf = sm.alloc("w1_bf", [128, KD * FF], BF16)
    w2_bf = sm.alloc("w2_bf", [128, KF * DIM], BF16)
    hmid_bf = sm.alloc("hmid_bf", [128, KF * TILE], BF16)
    ln2t = sm.alloc("ln2t", [128, KD * TILE], BF16)
    ffn_sb = sm.alloc("ffn_sb", [128, KD * TILE], F32)
    sm.reset()
    acc_ld = sm.alloc("acc_ld", [128, KD * TILE], F32)
    x2_ld = sm.alloc("x2_ld", [128, KD * TILE], F32)
    acc_tm = sm.alloc("acc_tm", [128, DIM], F32)
    x2_tm = sm.alloc("x2_tm", [128, DIM], F32)
    out_tm = sm.alloc("out_tm", [128, 2 * DIM], F32)
    trash3b = sm.alloc("trash3b", [128, DIM], BF16)
    sm.reset()

    pb = [nc.alloc_psum_tensor(f"pb{i}", [128, 512], F32) for i in range(8)]
    pbbf = [h.ap().bitcast(BF16) for h in pb]

    sems = {n: Sem(nc.alloc_semaphore(n)) for n in
            ("qS", "qA", "qG", "qW", "qV", "qX0", "qX1", "qT", "tS", "vS", "sS", "gS", "yS",
             "qWo", "qY0", "qY1", "qZ0", "qZ1", "qL", "qF", "qAC", "qR1", "qR2", "qO")}
    qS, qA, qG, qW, qV = (sems["qS"], sems["qA"], sems["qG"], sems["qW"], sems["qV"])
    qX = [sems["qX0"], sems["qX1"]]
    qT = sems["qT"]
    tS, vS, sS, gS = sems["tS"], sems["vS"], sems["sS"], sems["gS"]
    QSEM = {"sync": qS, "scalar": qA, "gpsimd": qG}
    ESEM = {"tensor": tS, "vector": vS, "scalar": sS, "gpsimd": gS, "sync": sems["yS"]}

    p = PB(nc, strict=strict)

    def dma(queue, dst, src, wait=(), accum=False, slow=False, transpose=False, sem=None):
        if queue == "sync#V":
            queue, sem = "sync", qV
        q = sem if sem is not None else QSEM[queue]
        kw = {}
        if accum:
            kw["accum_op"] = ALU.add
        if slow:
            kw["allow_slow_non_contiguous"] = True
        if transpose:
            return p.emit(queue, lambda e, dst=dst, src=src: e.dma_start_transpose(dst, src),
                          wait=wait, inc=q, inc_n=16)
        return p.emit(queue, lambda e, dst=dst, src=src, kw=kw: e.dma_start(out=dst, in_=src, **kw),
                      wait=wait, inc=q, inc_n=16)

    def cc(j, n=128):
        return c_consts[0:n, j:j + 1]

    bank_reads = [[] for _ in range(8)]
    bank_pe = [None] * 8

    def pe(i, fn, wait=()):
        w = list(wait) + bank_reads[i]
        bank_reads[i] = []
        tok = p.emit("tensor", fn, wait=w, inc=tS)
        bank_pe[i] = tok
        return tok

    def bread(i, eng, fn, wait=()):
        w = list(wait) + ([bank_pe[i]] if bank_pe[i] is not None else [])
        tok = p.emit(eng, fn, wait=w, inc=ESEM[eng])
        bank_reads[i].append(tok)
        return tok

    def ev(eng, fn, wait=()):
        return p.emit(eng, fn, wait=wait, inc=ESEM[eng])

    ALL_Q = [sems[k] for k in ("qS", "qA", "qG", "qW", "qV", "qX0", "qX1", "qT")]

    def _finish():
        qall_ = [sems[k] for k in sems]
        p.emit("sync", None,
               wait=[(q, q.n) for q in qall_ if q.n > 0 and q.h.name.startswith("q")])
        nums_ = sorted(s.h.num for s in sems.values())
        srange_ = range(nums_[0], nums_[-1] + 1)
        nc.gpsimd.dma_reset(srange_)
        nc.gpsimd.sem_clear(srange_)
        nc.all_engine_barrier()
        with nc.Block() as block_:
            p.replay(block_)
        return nc

    def barrier(level):
        if not (barriers & level):
            return
        toks = [p.emit(e, None, inc=ESEM[e]) for e in PB.ENGS]
        targets = toks + [(q, q.n) for q in ALL_Q if q.n > 0]
        for e in PB.ENGS:
            p.emit(e, None, wait=targets)

    ld = {}
    ld["eye_bf"] = dma("sync#V", c_eye_bf[:], eye_bf, sem=qW)
    ld["eye_f"] = dma("sync#V", c_eye_f[:], eye_f, sem=qW)
    ld["triu"] = dma("sync#V", c_triu[:], triu_f, sem=qW)
    ld["ones_bf"] = dma("sync#V", c_ones_col_bf[:], ones_col_bf, sem=qW)
    ld["ones_f"] = dma("sync#V", c_ones_row_f[:], ones_row_f, sem=qW)
    ld["selh"] = dma("sync#V", c_selh[:], selh_f, sem=qW)
    ld["selc"] = dma("sync#V", c_selc[:], selc_f, sem=qW)
    ld["pair16"] = dma("sync#V", c_pair16[:], pair16_f, sem=qW)
    ld["dblrow"] = dma("sync#V", c_dblrow[:], dblrow_f, sem=qW)
    ld["convw"] = dma("sync", convw_cols[:], convw, sem=qV)
    ld["convb"] = dma("sync", convb_col[:], convb, sem=qV)
    ld["gnw"] = dma("sync", gnw_col[:], gnw, sem=qV)
    ld["gnb"] = dma("sync", gnb_col[:], gnb, sem=qV)
    ld["bdt"] = dma("sync", bdt_col[:], bdt, sem=qV)
    ld["loga"] = dma("sync", loga_col[:], loga, sem=qV)
    ld["dpar"] = dma("sync", dpar_col[:], dpar, sem=qV)
    for _d in range(delay_w):
        dma("gpsimd", acc_sp[0, _d % KD, :, :], w1[_d % NE, 0:128, 0:S], sem=qG)
    tok_wxT = [dma("gpsimd", wxT[k][:], Wx[k * 128:(k + 1) * 128, :], sem=qW) for k in range(KD)]
    tok_wzT = [dma("gpsimd", wzT[k][:], Wz[k * 128:(k + 1) * 128, :], sem=qW) for k in range(KD)]
    tok_wbT = [dma("gpsimd", wbT[k][:], Wb[k * 128:(k + 1) * 128, :], sem=qW) for k in range(KI)]
    tok_wcT = [dma("gpsimd", wcT[k][:], Wc[k * 128:(k + 1) * 128, :], sem=qW) for k in range(KI)]
    tok_wdtT = [dma("gpsimd", wdtT[k][:], Wdt[k * 128:(k + 1) * 128, :], sem=qW) for k in range(KI)]
    for _r in range(dup_w):
        for k in range(KD):
            dma("gpsimd", wxT[k][:], Wx[k * 128:(k + 1) * 128, :], sem=qW)
            dma("gpsimd", wzT[k][:], Wz[k * 128:(k + 1) * 128, :], sem=qW)
        for k in range(KI):
            dma("gpsimd", wbT[k][:], Wb[k * 128:(k + 1) * 128, :], sem=qW)
            dma("gpsimd", wcT[k][:], Wc[k * 128:(k + 1) * 128, :], sem=qW)
            dma("gpsimd", wdtT[k][:], Wdt[k * 128:(k + 1) * 128, :], sem=qW)
    qW_all = (qW, qW.n)

    dma("sync", c_consts[:], consts_in, sem=qV)
    dma("sync", zeros8[:], zf_in[0:HEADS, 0:128], sem=qV)
    dma("sync", junk8[:], zf_in[0:HEADS, 0:128], sem=qV)
    dma("sync", st_f32[:], zf_in, sem=qV)
    dma("sync", st_bf[:], zb_in, sem=qV)
    dma("sync", stats_sum[:], zf_in[:, 0:BC * KI], sem=qV)
    dma("sync", stats_sq[:], zf_in[:, 0:BC * KI], sem=qV)
    dma("sync", racc[:], zf_in[:, 0:BC * KD], sem=qV)
    dma("sync", b1_cols[:], b1_in, sem=qV)
    dma("sync", gate_sb[:], gate_in, sem=qV)
    dma("sync", b2T[:], b2t_in, sem=qV)
    qV_all = (qV, qV.n)
    tok_consts = ev("vector", lambda e: e.nop(), wait=[qW_all, qV_all])

    tok_a = ev("scalar", lambda e: e.activation(out=a_col[:], in_=loga_col[:], func=AF.Exp),
               wait=[qW_all, tok_consts])
    tok_a = ev("vector", lambda e: e.tensor_scalar(out=a_col[:], in0=a_col[:],
                                                   scalar1=-1.0, scalar2=None, op0=ALU.mult),
               wait=[tok_a])
    tok_dbc = ev("vector", lambda e: e.tensor_scalar(out=junk8[:], in0=junk8[:],
                                                     scalar1=0.0, scalar2=dpar_col[:],
                                                     op0=ALU.mult, op1=ALU.add),
                 wait=[qW_all, tok_consts])
    tok_dtr = pe(2, lambda e: e.transpose(pb[2][:, 0:8], junk8[:], c_eye_f[0:8, 0:8]),
                 wait=[tok_dbc, qV_all])
    tok_dT = bread(2, "vector", lambda e: e.tensor_copy(dT[:], pb[2][:, 0:8]))
    tok_junk0 = dma("sync", junk8[:], zf_in[0:HEADS, 0:128], wait=[tok_dtr], sem=qT)

    gates = {
        "x_tm": [None, None],
        "rtmpA": [None, None],
        "ln1_sub": [None, None],
        "ln1_fm": None,
        "xpc": [None, None],
        "xc_t": None, "xc_v": None,
        "bc": None,
        "ctil": None,
        "aline": None,
        "cvt": [None, None],
        "sz": None,
        "y": None,
        "ysq": None,
        "sq": None,
        "erow": None, "wexp": None, "am1": None, "decb": None,
        "state": [tok_consts] * (BC * HEADS),
        "mt": [None, None],
        "xtm": [None, None],
        "bw": [None, None],
        "ef": [None, None],
    }
    tap_toks = []

    def _flat(ts):
        out = []
        for t in ts:
            if t is None:
                continue
            if isinstance(t, tuple) and len(t) == 2 and isinstance(t[0], Sem):
                out.append(t)
            elif isinstance(t, tuple):
                out.extend(x for x in t if x is not None)
            else:
                out.append(t)
        return out

    for b in range(nb):
        for ti in range(nt):
            t0 = ti * TILE
            for s in range(NSUB):
                par = s % 2
                xv = x_tm[:, par * DIM:(par + 1) * DIM]
                t_ld = dma("sync", xv, x_in[b, t0 + s * 128:t0 + (s + 1) * 128, :],
                           sem=qX[par], wait=[gates["x_tm"][par]] if gates["x_tm"][par] else [])
                rb = 32 * par
                rs = 32 * par + 8
                t_red = ev("vector", lambda e, xv=xv, rb=rb: e.reduce_sum(
                    out=rtmp[:, rb:rb + 1], in_=xv, axis=AX.X),
                    wait=[t_ld, qW_all, qV_all])
                t_sq = ev("scalar", lambda e, xv=xv, rs=rs: e.activation(
                    out=sq_bf[:, 0:DIM], in_=xv, func=AF.Square,
                    accum_out=rtmp[:, rs:rs + 1]),
                    wait=[t_ld, tok_consts] +
                         [t for t in (gates["rtmpA"][par], gates["sq"]) if t])
                t_mu = ev("vector", lambda e, rb=rb: e.tensor_scalar(
                    out=rtmp[:, rb + 1:rb + 2], in0=rtmp[:, rb:rb + 1],
                    scalar1=1.0 / DIM, scalar2=None, op0=ALU.mult), wait=[t_red])
                t_m2 = ev("vector", lambda e, rb=rb: e.tensor_mul(
                    rtmp[:, rb + 2:rb + 3], rtmp[:, rb + 1:rb + 2], rtmp[:, rb + 1:rb + 2]),
                    wait=[])
                t_var = ev("vector", lambda e, rb=rb, rs=rs: e.scalar_tensor_tensor(
                    out=rtmp[:, rb + 3:rb + 4], in0=rtmp[:, rs:rs + 1],
                    scalar=1.0 / DIM, in1=rtmp[:, rb + 2:rb + 3],
                    op0=ALU.mult, op1=ALU.subtract), wait=[t_sq])
                gates["rtmpA"][par] = t_var
                t_lnv = ev("scalar", lambda e, rb=rb, rs=rs: e.activation(
                    out=rtmp[:, rs + 1:rs + 2], in_=rtmp[:, rb + 3:rb + 4],
                    func=AF.Ln, bias=cc(8)), wait=[t_var, tok_consts])
                t_r = ev("scalar", lambda e, rs=rs, s=s: e.activation(
                    out=ln1_cols[:, 2 * s:2 * s + 1], in_=rtmp[:, rs + 1:rs + 2],
                    func=AF.Exp, scale=cc(3)), wait=[tok_consts])
                t_B = ev("vector", lambda e, rb=rb, s=s: e.scalar_tensor_tensor(
                    out=ln1_cols[:, 2 * s + 1:2 * s + 2], in0=rtmp[:, rb + 1:rb + 2],
                    scalar=-1.0, in1=ln1_cols[:, 2 * s:2 * s + 1],
                    op0=ALU.mult, op1=ALU.mult), wait=[t_r])
                lv = ln1_sub[:, par * DIM:(par + 1) * DIM]
                if "xsub" in DT and b == 0 and ti == 0:
                    tap_toks.append(dma("sync", taps["xsub"][s], xv, wait=[t_red], sem=qT))
                    tap_toks.append(dma("sync", taps["stat"][s, :, 0:10],
                                        rtmp[:, rb:rb + 10], wait=[t_B], sem=qT))
                t_ap = ev("scalar", lambda e, xv=xv, lv=lv, s=s: e.activation(
                    out=lv, in_=xv, func=AF.Identity,
                    scale=ln1_cols[:, 2 * s:2 * s + 1], bias=ln1_cols[:, 2 * s + 1:2 * s + 2]),
                    wait=[t_B] + ([gates["ln1_sub"][par]] if gates["ln1_sub"][par] else []))
                gates["x_tm"][par] = t_ap
                t_tr = None
                for k in range(KD):
                    bk = 2 + (k % 2)
                    sl = k // 2
                    t_tr = pe(bk, lambda e, lv=lv, k=k, sl=sl, bk=bk: e.transpose(
                        pbbf[bk][:, sl * 128:(sl + 1) * 128],
                        lv[:, k * 128:(k + 1) * 128], c_eye_bf[:]),
                        wait=[t_ap])
                gates["ln1_sub"][par] = t_tr
                lfv = ln1_fm.ap().rearrange("p (k4 two t) -> p k4 two t", k4=4, two=2)
                for par2 in range(2):
                    tok_fm = bread(2 + par2, "vector",
                                   lambda e, s=s, par2=par2, lfv=lfv: e.tensor_copy(
                                       lfv[:, :, par2, s * 128:(s + 1) * 128],
                                       pbbf[2 + par2][:, 0:512].rearrange(
                                           "p (k t) -> p k t", k=4)),
                                   wait=[gates["ln1_fm"]] if (s == 0 and par2 == 0 and gates["ln1_fm"]) else [])
            barrier(1)
            if "lnf" in DT:
                tap_toks.append(dma("sync",
                                    taps["lnf"][b, :, :, t0:t0 + TILE].rearrange("c p t -> p c t"),
                                    ln1_fm.ap().rearrange("p (c t) -> p c t", c=KD),
                                    wait=[tok_fm], sem=qT))
            tok_conv_last = None
            for m in range(KI):
                bk = m % 2
                par = m % 2
                t_mmx = pe(bk, lambda e, m=m, bk=bk: [
                    e.matmul(pb[bk][:, 0:512], wxT[k][:, m * 128:(m + 1) * 128],
                             ln1_fm[:, k * TILE:(k + 1) * TILE],
                             start=(k == 0), stop=(k == KD - 1)) for k in range(KD)][-1],
                    wait=[tok_fm, qW_all, qV_all] if m == 0 else [])
                xv = xpc[:, par * CW:(par + 1) * CW]
                t_xpc = bread(bk, "scalar", lambda e, xv=xv, bk=bk: e.activation(
                    out=xv[:, KER - 1:CW], in_=pb[bk][:, 0:512], func=AF.Copy),
                    wait=[gates["xpc"][par]] if gates["xpc"][par] else [])
                if ti == 0:
                    t_tf = ev("vector", lambda e, xv=xv: e.tensor_scalar(
                        out=xv[:, 0:KER - 1], in0=convb_col[:, 0:KER - 1],
                        scalar1=0.0, scalar2=None, op0=ALU.mult),
                              wait=[t_xpc, tok_consts])
                else:
                    t_tf = ev("vector", lambda e, xv=xv, m=m: e.tensor_copy(
                        xv[:, 0:KER - 1], xtail[:, m * (KER - 1):(m + 1) * (KER - 1)]),
                        wait=[t_xpc])
                cv = cvt[:, par * TILE:(par + 1) * TILE]
                t1 = ev("vector", lambda e, xv=xv, m=m, cv=cv: e.tensor_scalar(
                    out=cv, in0=xv[:, 0:TILE],
                    scalar1=convw_cols[:, m * KER:m * KER + 1], scalar2=None, op0=ALU.mult),
                    wait=[t_tf] + ([gates["cvt"][par]] if gates["cvt"][par] else []))
                for k in range(1, KER):
                    t1 = ev("vector", lambda e, xv=xv, m=m, k=k, cv=cv: e.scalar_tensor_tensor(
                        out=cv, in0=xv[:, k:k + TILE],
                        scalar=convw_cols[:, m * KER + k:m * KER + k + 1], in1=cv,
                        op0=ALU.mult, op1=ALU.add), wait=[])
                t_tl = ev("vector", lambda e, xv=xv, m=m: e.tensor_copy(
                    xtail[:, m * (KER - 1):(m + 1) * (KER - 1)], xv[:, TILE:TILE + KER - 1]),
                    wait=[])
                gates["xpc"][par] = t_tl
                wsil = [t1]
                if m == 0:
                    wsil += [t for t in (gates["xc_t"], gates["xc_v"]) if t]
                tok_sil = ev("scalar", lambda e, m=m, cv=cv: e.activation(
                    out=xc_bf[:, m * TILE:(m + 1) * TILE], in_=cv, func=AF.Silu,
                    bias=convb_col[:, m:m + 1]), wait=wsil)
                gates["cvt"][par] = tok_sil
                tok_conv_last = tok_sil
            for m in range(KI):
                bk = m % 2
                t_mmz = pe(bk, lambda e, m=m, bk=bk: [
                    e.matmul(pb[bk][:, 0:512], wzT[k][:, m * 128:(m + 1) * 128],
                             ln1_fm[:, k * TILE:(k + 1) * TILE],
                             start=(k == 0), stop=(k == KD - 1)) for k in range(KD)][-1])
                tok_sz = bread(bk, "scalar", lambda e, m=m, bk=bk: e.activation(
                    out=sz_bf[:, (m % 4) * TILE:(m % 4 + 1) * TILE],
                    in_=pb[bk][:, 0:512], func=AF.Silu),
                    wait=[gates["sz"]] if (m % 4 == 0 and gates["sz"]) else [])
                if m % 4 == 3:
                    g0 = (m // 4) * 4
                    gates["sz"] = dma(
                        "scalar", sz_sp[b, g0:g0 + 4, :, t0:t0 + TILE].rearrange("c p t -> p c t"),
                        sz_bf.ap().rearrange("p (c t) -> p c t", c=4),
                        wait=[tok_sz])
            gates["ln1_fm"] = t_mmz
            if "xc" in DT:
                tap_toks.append(dma("sync",
                                    taps["xc"][b, :, :, t0:t0 + TILE].rearrange("c p t -> p c t"),
                                    xc_bf.ap().rearrange("p (c t) -> p c t", c=KI),
                                    wait=[tok_conv_last], sem=qT))
            barrier(2)
            tok_bm = [None] * 4
            tok_cm = [None] * 4
            for j in range(4):
                bk = j % 2
                t_mmb = pe(bk, lambda e, j=j, bk=bk: [
                    e.matmul(pb[bk][:, 0:512], wbT[k][:, j * 128:(j + 1) * 128],
                             xc_bf[:, k * TILE:(k + 1) * TILE],
                             start=(k == 0), stop=(k == KI - 1)) for k in range(KI)][-1],
                    wait=([tok_conv_last, qW_all] if j == 0 else []))
                tok_bm[j] = bread(bk, "vector", lambda e, j=j, bk=bk: e.tensor_copy(
                    bmat[:, j * TILE:(j + 1) * TILE], pb[bk][:, 0:512]),
                    wait=[gates["bc"]] if (j == 0 and gates["bc"]) else [])
            for j in range(4):
                bk = j % 2
                t_mmc = pe(bk, lambda e, j=j, bk=bk: [
                    e.matmul(pb[bk][:, 0:512], wcT[k][:, j * 128:(j + 1) * 128],
                             xc_bf[:, k * TILE:(k + 1) * TILE],
                             start=(k == 0), stop=(k == KI - 1)) for k in range(KI)][-1])
                tok_cm[j] = bread(bk, "vector", lambda e, j=j, bk=bk: e.tensor_copy(
                    cmat[:, j * TILE:(j + 1) * TILE], pb[bk][:, 0:512]))
            t_mmd = pe(4, lambda e: [
                e.matmul(pb[4][0:HEADS, 0:512], wdtT[k][:, 0:HEADS],
                         xc_bf[:, k * TILE:(k + 1) * TILE],
                         start=(k == 0), stop=(k == KI - 1)) for k in range(KI)][-1])
            t_x = bread(4, "vector", lambda e: e.tensor_scalar(
                out=dt_sc[:, 0:TILE], in0=pb[4][0:HEADS, 0:512], scalar1=bdt_col[:],
                scalar2=None, op0=ALU.add),
                wait=[qW_all] + ([gates["aline"]] if gates["aline"] else []))
            t_ab = ev("scalar", lambda e: e.activation(
                out=dt_sc[:, TILE:2 * TILE], in_=dt_sc[:, 0:TILE],
                func=AF.Abs), wait=[t_x])
            t_e = ev("scalar", lambda e: e.activation(
                out=dt_sc[:, TILE:2 * TILE], in_=dt_sc[:, TILE:2 * TILE],
                func=AF.Exp, scale=cc(2, HEADS)), wait=[])
            t_l = ev("scalar", lambda e: e.activation(
                out=dt_sc[:, 2 * TILE:3 * TILE], in_=dt_sc[:, TILE:2 * TILE],
                func=AF.Ln, bias=cc(1, HEADS)), wait=[tok_consts])
            t_rl = ev("vector", lambda e: e.tensor_single_scalar(
                out=dt_sc[:, TILE:2 * TILE], in_=dt_sc[:, 0:TILE], scalar=0.0,
                op=ALU.max), wait=[t_l])
            tok_dt = ev("vector", lambda e: e.tensor_add(
                dt_fm[:], dt_sc[:, TILE:2 * TILE], dt_sc[:, 2 * TILE:3 * TILE]),
                wait=[])
            if "bcdt" in DT:
                tap_toks.append(dma("sync",
                                    taps["bmat"][b, :, :, t0:t0 + TILE].rearrange("c p t -> p c t"),
                                    bmat.ap().rearrange("p (c t) -> p c t", c=4),
                                    wait=[tok_bm[3]], sem=qT))
                tap_toks.append(dma("sync", taps["dt"][b, :, t0:t0 + TILE], dt_fm[:],
                                    wait=[tok_dt], sem=qT))
            barrier(4)
            t_ye = None
            for c in range(NCH):
                barrier(8)
                cs = c * 128
                cp = c % 2
                smb = cp * 32
                bsc = 2 + cp
                bet = 4 + cp
                t_dta = ev("vector", lambda e, cs=cs: e.tensor_scalar(
                    out=rtmp8[:, 0:128], in0=dt_fm[:, cs:cs + 128], scalar1=a_col[:],
                    scalar2=None, op0=ALU.mult),
                    wait=[tok_dt, tok_a] + ([gates["aline"]] if (c == 0 and gates["aline"]) else []))
                t_al = ev("vector", lambda e, cs=cs: e.tensor_tensor_scan(
                    out=alpha[:, cs:cs + 128], data0=rtmp8[:, 0:128], data1=zeros8[:],
                    initial=0.0, op0=ALU.add, op1=ALU.add), wait=[tok_consts])
                t_er = ev("scalar", lambda e, cs=cs: e.activation(
                    out=erow[:], in_=alpha[:, cs:cs + 128], func=AF.Exp),
                    wait=[t_al] + ([gates["erow"]] if gates["erow"] else []))
                t_am1 = ev("vector", lambda e, cs=cs: e.tensor_scalar(
                    out=alpham1[:], in0=alpha[:, cs:cs + 128],
                    scalar1=alpha[:, cs + 127:cs + 128], scalar2=None, op0=ALU.subtract),
                    wait=[gates["am1"]] if gates["am1"] else [])
                t_wex = ev("scalar", lambda e: e.activation(
                    out=wexp8[:], in_=alpham1[:], func=AF.Exp, scale=cc(2, HEADS)),
                    wait=[t_am1, tok_consts] + ([gates["wexp"]] if gates["wexp"] else []))
                t_dec = ev("scalar", lambda e, cs=cs: e.activation(
                    out=rtmp8[:, 128:129], in_=alpha[:, cs + 127:cs + 128], func=AF.Exp),
                    wait=[gates["decb"]] if gates["decb"] else [])
                t_decb = ev("vector", lambda e: e.tensor_scalar(
                    out=decb8[:], in0=junk8[:], scalar1=0.0,
                    scalar2=rtmp8[:, 128:129], op0=ALU.mult, op1=ALU.add),
                    wait=[t_dec, tok_junk0])
                gates["decb"] = t_decb
                t_t1 = pe(bsc, lambda e, cs=cs, bsc=bsc: e.transpose(
                    pb[bsc][:, 0:8], alpha[:, cs:cs + 128], c_eye_f[0:8, 0:8]), wait=[t_al])
                t_t2 = pe(bsc, lambda e, bsc=bsc: e.transpose(
                    pb[bsc][:, 8:16], wexp8[:], c_eye_f[0:8, 0:8]), wait=[t_wex])
                gates["wexp"] = t_t2
                t_t3 = pe(bsc, lambda e, cs=cs, bsc=bsc: e.transpose(
                    pb[bsc][:, 16:24], dt_fm[:, cs:cs + 128], c_eye_f[0:8, 0:8]), wait=[tok_dt])
                t_t4 = pe(bsc, lambda e, bsc=bsc: e.transpose(
                    pb[bsc][:, 24:32], decb8[:], c_eye_f[0:8, 0:8]), wait=[t_decb])
                t_sm = bread(bsc, "vector", lambda e, smb=smb, bsc=bsc: e.tensor_copy(
                    scn_sm[:, smb:smb + 32], pb[bsc][:, 0:32]))
                t_sm = ev("vector", lambda e, smb=smb: e.tensor_scalar(
                    out=scn_sm[:, smb:smb + 8], in0=scn_sm[:, smb:smb + 8],
                    scalar1=30.0, scalar2=None, op0=ALU.subtract), wait=[])
                tok_ct = None
                for j in range(4):
                    t_et = pe(bet, lambda e, j=j, bet=bet: e.matmul(
                        pb[bet][:, j * 128:(j + 1) * 128], c_selc[:, j * 128:(j + 1) * 128],
                        erow[:], start=True, stop=True),
                        wait=[t_er] if j == 0 else [])
                    tok_ct = bread(bet, "vector", lambda e, j=j, cs=cs, bet=bet: e.tensor_mul(
                        ctil[:, j * TILE + cs:j * TILE + cs + 128],
                        cmat[:, j * TILE + cs:j * TILE + cs + 128],
                        pb[bet][:, j * 128:(j + 1) * 128]),
                        wait=[tok_cm[j]] + ([gates["ctil"]] if (j == 0 and c == 0 and gates["ctil"]) else []))
                gates["erow"] = t_et
                for h in range(HEADS):
                    barrier(16)
                    hp = h % 2
                    j, rr = h // 2, (h % 2) * STATE
                    bT = 0 + hp
                    bG = 4 + hp
                    bY = 6 + hp
                    bh = b * HEADS + h
                    so = (b * 4 + h // 2) * HD
                    t_gt = pe(bG, lambda e, j=j, rr=rr, cs=cs, bG=bG: e.matmul(
                        pb[bG][:, 0:128],
                        bmat[rr:rr + STATE, j * TILE + cs:j * TILE + cs + 128],
                        cmat[rr:rr + STATE, j * TILE + cs:j * TILE + cs + 128],
                        start=True, stop=True),
                        wait=[tok_bm[j], tok_cm[j]])
                    t_d0 = pe(bG, lambda e, h=h, cs=cs, bG=bG: e.matmul(
                        pb[bG][:, 128:256], c_selh[:, h * 128:(h + 1) * 128],
                        alpha[:, cs:cs + 128], start=True, stop=True),
                        wait=[t_al])
                    t_msk = bread(bG, "vector", lambda e, h=h, hp=hp, smb=smb, bG=bG:
                                  e.scalar_tensor_tensor(
                                      out=e_f32[:, hp * 128:(hp + 1) * 128],
                                      in0=pb[bG][:, 128:256],
                                      scalar=scn_sm[:, smb + h:smb + h + 1],
                                      in1=c_triu[:], op0=ALU.subtract, op1=ALU.mult),
                                  wait=[t_sm] + ([gates["ef"][hp]] if gates["ef"][hp] else []))
                    t_exp = ev("scalar", lambda e, hp=hp: e.activation(
                        out=e_f32[:, hp * 128:(hp + 1) * 128],
                        in_=e_f32[:, hp * 128:(hp + 1) * 128], func=AF.Exp, bias=cc(0)),
                        wait=[t_msk, tok_consts])
                    t_mt = bread(bG, "vector", lambda e, hp=hp, bG=bG: e.tensor_mul(
                        mt_bf[:, hp * 128:(hp + 1) * 128], pb[bG][:, 0:128],
                        e_f32[:, hp * 128:(hp + 1) * 128]),
                        wait=[t_exp] + ([gates["mt"][hp]] if gates["mt"][hp] else []))
                    gates["ef"][hp] = t_mt
                    t_xt = pe(bT, lambda e, h=h, cs=cs, bT=bT: e.transpose(
                        pbbf[bT][:, 0:128], xc_bf[:, (2 * h) * TILE + cs:(2 * h) * TILE + cs + 128],
                        c_eye_bf[:]), wait=[tok_conv_last])
                    t_xt = pe(bT, lambda e, h=h, cs=cs, bT=bT: e.transpose(
                        pbbf[bT][:, 128:256], xc_bf[:, (2 * h + 1) * TILE + cs:(2 * h + 1) * TILE + cs + 128],
                        c_eye_bf[:]))
                    t_bt = pe(bT, lambda e, j=j, rr=rr, cs=cs, bT=bT: e.transpose(
                        pbbf[bT][:, 256:256 + STATE],
                        bmat[rr:rr + STATE, j * TILE + cs:j * TILE + cs + 128],
                        c_eye_bf[rr:rr + STATE, rr:rr + STATE]), wait=[tok_bm[j]])
                    t_xtm = bread(bT, "vector", lambda e, hp=hp, smb=smb, h=h, bT=bT: e.tensor_scalar(
                        out=xtm_bf[:, hp * HD:(hp + 1) * HD], in0=pbbf[bT][:, 0:256],
                        scalar1=scn_sm[:, smb + 16 + h:smb + 17 + h], scalar2=None,
                        op0=ALU.mult),
                        wait=[t_sm] + ([gates["xtm"][hp]] if gates["xtm"][hp] else []))
                    t_bw = bread(bT, "vector", lambda e, hp=hp, smb=smb, h=h, bT=bT: e.tensor_scalar(
                        out=bw_bf[:, hp * STATE:(hp + 1) * STATE], in0=pbbf[bT][:, 256:256 + STATE],
                        scalar1=scn_sm[:, smb + 8 + h:smb + 9 + h], scalar2=None,
                        op0=ALU.mult),
                        wait=[gates["bw"][hp]] if gates["bw"][hp] else [])
                    def ymms(e, hp=hp, cs=cs, j=j, rr=rr, so=so, bY=bY):
                        if scan_mask & 1:
                            e.matmul(pb[bY][:, 0:128], xtm_bf[:, hp * HD:hp * HD + 128],
                                     mt_bf[:, hp * 128:(hp + 1) * 128],
                                     start=True, stop=not (scan_mask & 2))
                        if scan_mask & 2:
                            e.matmul(pb[bY][:, 0:128], st_bf[rr:rr + STATE, so:so + 128],
                                     ctil[rr:rr + STATE, j * TILE + cs:j * TILE + cs + 128],
                                     start=not (scan_mask & 1), stop=True)
                        if scan_mask & 1:
                            e.matmul(pb[bY][:, 128:256], xtm_bf[:, hp * HD + 128:hp * HD + 256],
                                     mt_bf[:, hp * 128:(hp + 1) * 128],
                                     start=True, stop=not (scan_mask & 2))
                        if scan_mask & 2:
                            e.matmul(pb[bY][:, 128:256], st_bf[rr:rr + STATE, so + 128:so + 256],
                                     ctil[rr:rr + STATE, j * TILE + cs:j * TILE + cs + 128],
                                     start=not (scan_mask & 1), stop=True)
                        if rr == 0:
                            return e.matmul(pb[bY][0:STATE, 256:512],
                                            bw_bf[:, hp * STATE:(hp + 1) * STATE],
                                            xtm_bf[:, hp * HD:(hp + 1) * HD],
                                            start=True, stop=True)
                        return e.matmul(pb[bY][rr:rr + STATE, 256:512],
                                        bw_bf[:, hp * STATE:(hp + 1) * STATE],
                                        xtm_bf[:, hp * HD:(hp + 1) * HD],
                                        start=True, stop=True, tile_position=(0, rr))
                    t_y = pe(bY, ymms, wait=[t_xtm, t_bw, t_mt, tok_ct, gates["state"][bh]])
                    gates["xtm"][hp] = t_y
                    gates["bw"][hp] = t_y
                    gates["mt"][hp] = t_y
                    t_su = bread(bY, "vector", lambda e, so=so, rr=rr, smb=smb, h=h, bY=bY:
                                 e.scalar_tensor_tensor(
                                     out=st_f32[rr:rr + STATE, so:so + HD],
                                     in0=st_f32[rr:rr + STATE, so:so + HD],
                                     scalar=scn_sm[rr:rr + STATE, smb + 24 + h:smb + 25 + h],
                                     in1=pb[bY][rr:rr + STATE, 256:512],
                                     op0=ALU.mult, op1=ALU.add))
                    gates["state"][bh] = ev("vector", lambda e, so=so, rr=rr: e.tensor_copy(
                        st_bf[rr:rr + STATE, so:so + HD], st_f32[rr:rr + STATE, so:so + HD]),
                        wait=[])
                    for pi in range(2):
                        t_ye = bread(bY, "vector", lambda e, h=h, pi=pi, cs=cs, bY=bY:
                                     e.scalar_tensor_tensor(
                                         out=y_sb[:, (2 * h + pi) * 128:(2 * h + pi) * 128 + 128],
                                         in0=xc_bf[:, (2 * h + pi) * TILE + cs:
                                                   (2 * h + pi) * TILE + cs + 128],
                                         scalar=dT[:, h:h + 1] if (scan_mask & 4) else cc(1),
                                         in1=pb[bY][:, pi * 128:(pi + 1) * 128],
                                         op0=ALU.mult if (scan_mask & 4) else ALU.bypass,
                                         op1=ALU.add),
                                     wait=[tok_dT] +
                                          (_flat([gates["y"], gates["ysq"]])
                                           if (h == 0 and pi == 0) else []))
                yv = y_sb[:]
                gates["y"] = dma("sync",
                                 y_sp[b, :, :, t0 + cs:t0 + cs + 128].rearrange("c p t -> p c t"),
                                 yv.rearrange("p (c t) -> p c t", c=KI),
                                 wait=[t_ye])
                if "y" in DT:
                    gates["y2"] = tap_toks[-0:] and None
                    _ytap = dma("sync",
                                taps["y"][b, :, :, t0 + cs:t0 + cs + 128].rearrange("c p t -> p c t"),
                                yv.rearrange("p (c t) -> p c t", c=KI),
                                wait=[t_ye], sem=qT)
                    tap_toks.append(_ytap)
                    gates["y"] = (gates["y"], _ytap)
                t_st1 = ev("vector", lambda e, yv=yv: e.reduce_sum(
                    out=rtmp[:, 48:64], in_=yv.rearrange("p (c t) -> p c t", c=KI),
                    axis=AX.X), wait=[])
                t_st2 = ev("vector", lambda e, b=b: e.tensor_add(
                    stats_sum[:, b * KI:(b + 1) * KI], stats_sum[:, b * KI:(b + 1) * KI],
                    rtmp[:, 48:64]), wait=[tok_consts])
                t_sq1 = ev("scalar", lambda e, yv=yv: e.activation(
                    out=sq_bf[:], in_=yv, func=AF.Square),
                    wait=[t_ye] + ([gates["sq"]] if gates["sq"] else []))
                gates["ysq"] = t_sq1
                t_sq2 = ev("vector", lambda e: e.reduce_sum(
                    out=rtmp[:, 16:32], in_=sq_bf.ap().rearrange("p (c t) -> p c t", c=KI),
                    axis=AX.X), wait=[t_sq1])
                gates["sq"] = t_sq2
                t_sq3 = ev("vector", lambda e, b=b: e.tensor_add(
                    stats_sq[:, b * KI:(b + 1) * KI], stats_sq[:, b * KI:(b + 1) * KI],
                    rtmp[:, 16:32]), wait=[tok_consts])
            gates["xc_t"] = t_xt
            gates["xc_v"] = t_ye
            gates["bc"] = t_y
            gates["ctil"] = t_y
            gates["aline"] = t_y
            gates["am1"] = t_t2

    t_statv = p.last["vector"]
    for b in range(nb):
        t_tr1 = pe(2, lambda e, b=b: e.transpose(
            pb[2][0:KI, 0:128], stats_sum[:, b * KI:(b + 1) * KI], c_eye_f[:]),
            wait=[t_statv])
        t_tr2 = pe(2, lambda e, b=b: e.transpose(
            pb[2][0:KI, 128:256], stats_sq[:, b * KI:(b + 1) * KI], c_eye_f[:]))
        t_r1 = bread(2, "vector", lambda e: e.reduce_sum(
            out=rtmp[0:KI, 48:49], in_=pb[2][0:KI, 0:128], axis=AX.X))
        t_r2 = bread(2, "vector", lambda e: e.reduce_sum(
            out=rtmp[0:KI, 49:50], in_=pb[2][0:KI, 128:256], axis=AX.X))
        t_pm = pe(3, lambda e: e.matmul(pb[3][0:HEADS, 0:2], c_pair16[:, 0:HEADS],
                                        rtmp[0:KI, 48:50], start=True, stop=True),
                  wait=[t_r1, t_r2])
        t_ev = bread(3, "vector", lambda e: e.tensor_copy(rtmp8[:, 130:132], pb[3][0:HEADS, 0:2]))
        ev("vector", lambda e: e.tensor_scalar(out=rtmp8[:, 132:133], in0=rtmp8[:, 130:131],
                                               scalar1=1.0 / NORM_GN, scalar2=None, op0=ALU.mult))
        ev("vector", lambda e: e.tensor_scalar(out=rtmp8[:, 133:134], in0=rtmp8[:, 131:132],
                                               scalar1=1.0 / NORM_GN, scalar2=None, op0=ALU.mult))
        ev("vector", lambda e: e.tensor_mul(rtmp8[:, 134:135], rtmp8[:, 132:133], rtmp8[:, 132:133]))
        t_v8 = ev("vector", lambda e: e.tensor_sub(rtmp8[:, 135:136], rtmp8[:, 133:134],
                                                   rtmp8[:, 134:135]))
        t_l8 = ev("scalar", lambda e: e.activation(out=rtmp8[:, 136:137], in_=rtmp8[:, 135:136],
                                                   func=AF.Ln, bias=cc(8, HEADS)), wait=[t_v8])
        t_r8 = ev("scalar", lambda e: e.activation(out=rtmp8[:, 137:138], in_=rtmp8[:, 136:137],
                                                   func=AF.Exp, scale=cc(3, HEADS)))
        t_m8 = ev("vector", lambda e: e.tensor_mul(rtmp8[:, 138:139], rtmp8[:, 132:133],
                                                   rtmp8[:, 137:138]), wait=[t_r8])
        t_x1 = pe(3, lambda e: e.matmul(pb[3][0:1, 16:32], rtmp8[0:HEADS, 137:138],
                                        c_dblrow[:], start=True, stop=True),
                  wait=[t_m8])
        t_x2m = pe(3, lambda e: e.matmul(pb[3][0:1, 32:48], rtmp8[0:HEADS, 138:139],
                                         c_dblrow[:], start=True, stop=True))
        t_rr = bread(3, "vector", lambda e: e.tensor_copy(rtmp[0:1, 16:32], pb[3][0:1, 16:32]))
        t_rr = bread(3, "vector", lambda e: e.tensor_copy(rtmp[0:1, 32:48], pb[3][0:1, 32:48]))
        t_bc1 = pe(4, lambda e: e.matmul(pb[4][:, 0:16], c_ones_row_f[:], rtmp[0:1, 16:32],
                                         start=True, stop=True), wait=[t_rr])
        t_bc2 = pe(4, lambda e: e.matmul(pb[4][:, 16:32], c_ones_row_f[:], rtmp[0:1, 32:48],
                                         start=True, stop=True))
        bread(4, "vector", lambda e, b=b: e.tensor_mul(
            gn_scale[:, b * KI:(b + 1) * KI], gnw_col[:], pb[4][:, 0:16]))
        bread(4, "vector", lambda e: e.tensor_mul(rtmp[:, 48:64], gnw_col[:], pb[4][:, 16:32]))
        ev("vector", lambda e, b=b: e.tensor_sub(
            gn_bias[:, b * KI:(b + 1) * KI], gnb_col[:], rtmp[:, 48:64]))
    t_b2T = tok_consts
    if stop_after == "p15":
        return _finish()

    qWo, qY = sems["qWo"], [sems["qY0"], sems["qY1"]]
    qZ = [sems["qZ0"], sems["qZ1"]]
    qL, qF, qAC = sems["qL"], sems["qF"], sems["qAC"]
    qR1, qR2, qO = sems["qR1"], sems["qR2"], sems["qO"]

    j_p1 = [gates["aline"], p.last["vector"], p.last["scalar"],
            (qS, qS.n), (qA, qA.n), (qX[0], qX[0].n), (qX[1], qX[1].n),
            (qT, qT.n), (qG, qG.n)]
    tok_wo = None
    for k in range(KI):
        tok_wo = dma("gpsimd", woutT[:, k * DIM:(k + 1) * DIM],
                     Wout[k * 128:(k + 1) * 128, :], sem=qWo,
                     wait=j_p1 if k == 0 else [])
    if stop_after == "p2_wo":
        return _finish()

    def p2stage(stage):
        if not (isinstance(stop_after, str) and stop_after.startswith("p2_")):
            return True
        order = ["gn", "x", "mm", "ln", "l2", "sp"]
        return order.index(stage) <= order.index(stop_after[3:])

    gates2 = {"yld": [None, None], "szld": [None, None], "x2tm": [None, None],
              "xfm": None, "u": None, "attn": None, "x2spill": None,
              "ln2spill": None, "x2f": None, "ln2b": None, "sqr": None}
    tok_x2sp = []
    tok_ln2sp = []
    for b in range(nb):
        for ti in range(nt):
            t0 = ti * TILE
            t_y4 = [None, None]
            t_z4 = [None, None]
            for g in range(2):
                t_y4[g] = dma("sync", y_ld[:, g * 4 * TILE:(g + 1) * 4 * TILE].rearrange(
                    "p (c t) -> p c t", c=4),
                    y_sp[b, g * 4:(g + 1) * 4, :, t0:t0 + TILE].rearrange("c p t -> p c t"),
                    sem=qY[g], wait=(j_p1 if (b == 0 and ti == 0) else []) +
                                    ([gates2["yld"][g]] if gates2["yld"][g] else []))
                t_z4[g] = dma("sync", sz_ld[:, g * 4 * TILE:(g + 1) * 4 * TILE].rearrange(
                    "p (c t) -> p c t", c=4),
                    sz_sp[b, g * 4:(g + 1) * 4, :, t0:t0 + TILE].rearrange("c p t -> p c t"),
                    sem=qZ[g], wait=[gates2["szld"][g]] if gates2["szld"][g] else [])
            t_y4b = [None, None]
            t_z4b = [None, None]
            t_uc = [None] * KI
            for c in range(KI):
                g = (c // 4) % 2
                if c == 8:
                    for g2 in range(2):
                        t_y4b[g2] = dma("sync", y_ld[:, g2 * 4 * TILE:(g2 + 1) * 4 * TILE].rearrange(
                            "p (c t) -> p c t", c=4),
                            y_sp[b, 8 + g2 * 4:8 + (g2 + 1) * 4, :, t0:t0 + TILE].rearrange("c p t -> p c t"),
                            sem=qY[g2], wait=[t_uc[3 + g2 * 4]])
                        t_z4b[g2] = dma("sync", sz_ld[:, g2 * 4 * TILE:(g2 + 1) * 4 * TILE].rearrange(
                            "p (c t) -> p c t", c=4),
                            sz_sp[b, 8 + g2 * 4:8 + (g2 + 1) * 4, :, t0:t0 + TILE].rearrange("c p t -> p c t"),
                            sem=qZ[g2], wait=[t_uc[3 + g2 * 4]])
                yw = (t_y4b[g] if c >= 8 else t_y4[g])
                zw = (t_z4b[g] if c >= 8 else t_z4[g])
                sl = (c % 8) * TILE
                t_gn = ev("vector", lambda e, c=c, b=b, sl=sl: e.tensor_scalar(
                    out=gnt_bf[:], in0=y_ld[:, sl:sl + TILE],
                    scalar1=gn_scale[:, b * KI + c:b * KI + c + 1],
                    scalar2=gn_bias[:, b * KI + c:b * KI + c + 1],
                    op0=ALU.mult, op1=ALU.add), wait=[yw, t_b2T])
                t_uc[c] = ev("vector", lambda e, c=c, sl=sl: e.tensor_mul(
                    u_bf[:, c * TILE:(c + 1) * TILE], gnt_bf[:], sz_ld[:, sl:sl + TILE]),
                    wait=[zw])
            if not p2stage("x"):
                continue
            t_xf = None
            for s in range(NSUB):
                par = s % 2
                xv = x_tm2[:, par * DIM:(par + 1) * DIM]
                t_xl = dma("sync", xv, x_in[b, t0 + s * 128:t0 + (s + 1) * 128, :],
                           sem=qX[par], wait=[gates2["x2tm"][par]] if gates2["x2tm"][par] else [])
                t_tt = None
                for k in range(KD):
                    bk = 6 + (k % 2)
                    slq = k // 2
                    t_tt = pe(bk, lambda e, xv=xv, k=k, slq=slq, bk=bk: e.transpose(
                        pb[bk][:, slq * 128:(slq + 1) * 128],
                        xv[:, k * 128:(k + 1) * 128], c_eye_f[:]),
                        wait=[t_xl] if k == 0 else [])
                gates2["x2tm"][par] = t_tt
                xfv = x_fm.ap().rearrange("p (k4 half t) -> p k4 half t", k4=4, half=2)
                for half in range(2):
                    t_xf = bread(6 + half, "vector",
                                 lambda e, s=s, half=half, xfv=xfv: e.tensor_copy(
                                     xfv[:, :, half, s * 128:(s + 1) * 128],
                                     pb[6 + half][:, 0:512].rearrange("p (k t) -> p k t", k=4)),
                                 wait=[gates2["xfm"]] if (s == 0 and half == 0 and gates2["xfm"]) else [])
            if not p2stage("mm"):
                continue
            t_att = [None] * KD
            for o in range(KD):
                bk = o % 2
                t_mm = pe(bk, lambda e, o=o, bk=bk: [
                    e.matmul(pb[bk][:, 0:512], woutT[:, k * DIM + o * 128:k * DIM + (o + 1) * 128],
                             u_bf[:, k * TILE:(k + 1) * TILE],
                             start=(k == 0), stop=(k == KI - 1)) for k in range(KI)][-1],
                    wait=[t_uc[KI - 1], tok_wo] if o == 0 else [])
                t_af = bread(bk, "vector", lambda e, o=o, bk=bk: e.tensor_copy(
                    attn_f[:, o * TILE:(o + 1) * TILE], pb[bk][:, 0:512]),
                    wait=[gates2["attn"]] if (o == 0 and gates2["attn"]) else [])
                t_ab2 = ev("scalar", lambda e, o=o: e.activation(
                    out=attn_bf[:, o * TILE:(o + 1) * TILE],
                    in_=attn_f[:, o * TILE:(o + 1) * TILE],
                    func=AF.Identity), wait=[t_af])
                t_sq2 = ev("scalar", lambda e, o=o: e.activation(
                    out=sq2_bf[:, o * TILE:(o + 1) * TILE],
                    in_=attn_f[:, o * TILE:(o + 1) * TILE],
                    func=AF.Square))
                t_att[o] = t_sq2
            t_s1 = pe(2, lambda e: [
                e.matmul(pb[2][0:1, 0:512], c_ones_col_bf[:], attn_bf[:, k * TILE:(k + 1) * TILE],
                         start=(k == 0), stop=(k == KD - 1)) for k in range(KD)][-1],
                wait=[t_att[KD - 1]])
            t_s2 = pe(3, lambda e: [
                e.matmul(pb[3][0:1, 0:512], c_ones_col_bf[:], sq2_bf[:, k * TILE:(k + 1) * TILE],
                         start=(k == 0), stop=(k == KD - 1)) for k in range(KD)][-1])

            def ln_rows(bmu, bsq):
                bread(bmu, "vector", lambda e: e.tensor_scalar(
                    out=rows_sb[:, 0:TILE], in0=pb[bmu][0:1, 0:512],
                    scalar1=1.0 / DIM, scalar2=None, op0=ALU.mult))
                ev("vector", lambda e: e.tensor_mul(rows_sb[:, TILE:2 * TILE],
                                                    rows_sb[:, 0:TILE], rows_sb[:, 0:TILE]))
                bread(bsq, "vector", lambda e: e.scalar_tensor_tensor(
                    out=rows_sb[:, TILE:2 * TILE], in0=pb[bsq][0:1, 0:512],
                    scalar=1.0 / DIM, in1=rows_sb[:, TILE:2 * TILE],
                    op0=ALU.mult, op1=ALU.subtract))
                t_ln = ev("scalar", lambda e: e.activation(
                    out=rows_sb[:, TILE:2 * TILE], in_=rows_sb[:, TILE:2 * TILE],
                    func=AF.Ln, bias=cc(8, 1)), wait=[p.last["vector"]])
                t_A = ev("scalar", lambda e: e.activation(
                    out=rows_sb[:, 2 * TILE:3 * TILE], in_=rows_sb[:, TILE:2 * TILE],
                    func=AF.Exp, scale=cc(3, 1)))
                t_B2 = ev("vector", lambda e: e.scalar_tensor_tensor(
                    out=rows_sb[:, 3 * TILE:4 * TILE], in0=rows_sb[:, 0:TILE],
                    scalar=-1.0, in1=rows_sb[:, 2 * TILE:3 * TILE],
                    op0=ALU.mult, op1=ALU.mult), wait=[t_A])
                t_bA = pe(4, lambda e: e.matmul(pb[4][:, 0:512], c_ones_row_f[:],
                                                rows_sb[:, 2 * TILE:3 * TILE],
                                                start=True, stop=True), wait=[t_B2])
                t_bB = pe(5, lambda e: e.matmul(pb[5][:, 0:512], c_ones_row_f[:],
                                                rows_sb[:, 3 * TILE:4 * TILE],
                                                start=True, stop=True), wait=[t_B2])
                return t_bA, t_bB
            if not p2stage("ln"):
                continue
            ln_rows(2, 3)
            xfv2 = x_fm.ap().rearrange("p (k4 half t) -> p k4 half t", k4=4, half=2)
            t_x2 = None
            for o in range(KD):
                k4, half = o // 2, o % 2
                t_t1 = bread(4, "vector", lambda e, o=o: e.tensor_mul(
                    x2_f[:, o * TILE:(o + 1) * TILE], attn_f[:, o * TILE:(o + 1) * TILE],
                    pb[4][:, 0:512]), wait=[t_xf, gates2["x2f"]] if o == 0 else [])
                t_t2 = bread(5, "vector", lambda e, o=o: e.tensor_add(
                    x2_f[:, o * TILE:(o + 1) * TILE], x2_f[:, o * TILE:(o + 1) * TILE],
                    pb[5][:, 0:512]))
                t_x2 = ev("vector", lambda e, o=o, k4=k4, half=half, xfv2=xfv2: e.tensor_add(
                    x2_f[:, o * TILE:(o + 1) * TILE], x2_f[:, o * TILE:(o + 1) * TILE],
                    xfv2[:, k4, half, :]))
            gates2["xfm"] = t_x2
            if not p2stage("l2"):
                continue
            t_att2 = [None] * KD
            for o in range(KD):
                t_c1 = ev("scalar", lambda e, o=o: e.activation(
                    out=attn_bf[:, o * TILE:(o + 1) * TILE], in_=x2_f[:, o * TILE:(o + 1) * TILE],
                    func=AF.Identity), wait=[t_x2] if o == 0 else [])
                t_att2[o] = ev("scalar", lambda e, o=o: e.activation(
                    out=sq2_bf[:, o * TILE:(o + 1) * TILE], in_=x2_f[:, o * TILE:(o + 1) * TILE],
                    func=AF.Square))
            t_s1b = pe(2, lambda e: [
                e.matmul(pb[2][0:1, 0:512], c_ones_col_bf[:], attn_bf[:, k * TILE:(k + 1) * TILE],
                         start=(k == 0), stop=(k == KD - 1)) for k in range(KD)][-1],
                wait=[t_att2[KD - 1]])
            t_s2b = pe(3, lambda e: [
                e.matmul(pb[3][0:1, 0:512], c_ones_col_bf[:], sq2_bf[:, k * TILE:(k + 1) * TILE],
                         start=(k == 0), stop=(k == KD - 1)) for k in range(KD)][-1])
            ln_rows(2, 3)
            t_ln2 = None
            for o in range(KD):
                t_t1 = bread(4, "vector", lambda e, o=o: e.tensor_mul(
                    x2_f[:, o * TILE:(o + 1) * TILE] if False else attn_f[:, o * TILE:(o + 1) * TILE],
                    x2_f[:, o * TILE:(o + 1) * TILE], pb[4][:, 0:512]),
                    wait=[gates2["ln2b"]] if (o == 0 and gates2["ln2b"]) else [])
                t_ln2 = bread(5, "vector", lambda e, o=o: e.tensor_add(
                    ln2_bf[:, o * TILE:(o + 1) * TILE], attn_f[:, o * TILE:(o + 1) * TILE],
                    pb[5][:, 0:512]))
                ev("vector", lambda e, o=o: e.reduce_sum(
                    out=rtmp[:, 48 + o:49 + o], in_=ln2_bf[:, o * TILE:(o + 1) * TILE],
                    axis=AX.X))
                ev("vector", lambda e, o=o, b=b: e.tensor_add(
                    racc[:, b * KD + o:b * KD + o + 1], racc[:, b * KD + o:b * KD + o + 1],
                    rtmp[:, 48 + o:49 + o]))
            if not p2stage("sp"):
                continue
            gates2["x2spill"] = dma("scalar", x2_sp[b, :, :, t0:t0 + TILE].rearrange("c p t -> p c t"),
                                    x2_f.ap().rearrange("p (c t) -> p c t", c=KD),
                                    sem=qA, wait=[t_ln2])
            tok_x2sp.append(gates2["x2spill"])
            gates2["ln2spill"] = dma("sync", ln2_sp[b, :, :, t0:t0 + TILE].rearrange("c p t -> p c t"),
                                     ln2_bf.ap().rearrange("p (c t) -> p c t", c=KD),
                                     sem=qS, wait=[t_ln2])
            tok_ln2sp.append(gates2["ln2spill"])
            gates2["x2f"] = gates2["x2spill"]
            gates2["ln2b"] = gates2["ln2spill"]
            gates2["attn"] = t_ln2
            gates2["yld"] = [t_uc[11], t_uc[15]]
            gates2["szld"] = [t_uc[11], t_uc[15]]
            gates2["sqr"] = t_att2[KD - 1]

    if stop_after == "p2" or (isinstance(stop_after, str) and stop_after.startswith("p2_")):
        return _finish()

    for b in range(nb):
        t_rep = ev("vector", lambda e, b=b: e.tensor_scalar(
            out=rtmp[:, 16:24], in0=racc[:, b * KD:(b + 1) * KD],
            scalar1=1.0 / S, scalar2=None, op0=ALU.mult))
        t_lg = pe(2, lambda e: [
            e.matmul(pb[2][0:1, 0:8], rtmp[:, 16 + c:17 + c], gate_sb[:, c * 8:(c + 1) * 8],
                     start=(c == 0), stop=(c == KD - 1)) for c in range(KD)][-1],
            wait=[t_rep])
        t_lr = bread(2, "vector", lambda e: e.tensor_copy(r8_sb[:, 0:8], pb[2][0:1, 0:8]))
        t_mx = ev("vector", lambda e: e.reduce_max(out=r8_sb[:, 8:9], in_=r8_sb[:, 0:8], axis=AX.X))
        t_nm = ev("vector", lambda e: e.tensor_scalar(out=r8_sb[:, 9:10], in0=r8_sb[:, 8:9],
                                                      scalar1=-1.0, scalar2=None, op0=ALU.mult))
        t_e8 = ev("scalar", lambda e: e.activation(out=r8_sb[:, 16:24], in_=r8_sb[:, 0:8],
                                                   func=AF.Exp, bias=r8_sb[:, 9:10]),
                  wait=[t_nm])
        t_m1 = ev("vector", lambda e: e.reduce_max(out=r8_sb[:, 24:25], in_=r8_sb[:, 16:24],
                                                   axis=AX.X), wait=[t_e8])
        t_g1 = ev("vector", lambda e: e.tensor_scalar(out=r8_sb[:, 32:40], in0=r8_sb[:, 16:24],
                                                      scalar1=r8_sb[:, 24:25], scalar2=None,
                                                      op0=ALU.is_ge))
        t_mk = ev("vector", lambda e: e.scalar_tensor_tensor(
            out=r8_sb[:, 40:48], in0=r8_sb[:, 32:40], scalar=-1e30, in1=r8_sb[:, 16:24],
            op0=ALU.mult, op1=ALU.add))
        t_m2 = ev("vector", lambda e: e.reduce_max(out=r8_sb[:, 25:26], in_=r8_sb[:, 40:48],
                                                   axis=AX.X))
        t_g2 = ev("vector", lambda e: e.tensor_scalar(out=r8_sb[:, 48:56], in0=r8_sb[:, 16:24],
                                                      scalar1=r8_sb[:, 25:26], scalar2=None,
                                                      op0=ALU.is_ge))
        t_w8 = ev("vector", lambda e: e.tensor_mul(r8_sb[:, 56:64], r8_sb[:, 16:24],
                                                   r8_sb[:, 48:56]))
        t_sm8 = ev("vector", lambda e: e.reduce_sum(out=r8_sb[:, 26:27], in_=r8_sb[:, 56:64],
                                                    axis=AX.X))
        t_li = ev("scalar", lambda e: e.activation(out=r8_sb[:, 27:28], in_=r8_sb[:, 26:27],
                                                   func=AF.Ln), wait=[t_sm8])
        t_iv = ev("scalar", lambda e: e.activation(out=r8_sb[:, 28:29], in_=r8_sb[:, 27:28],
                                                   func=AF.Exp, scale=cc(2, 1)))
        t_cb = ev("vector", lambda e: e.tensor_scalar(out=r8_sb[:, 56:64], in0=r8_sb[:, 56:64],
                                                      scalar1=r8_sb[:, 28:29], scalar2=None,
                                                      op0=ALU.mult), wait=[t_iv])
        t_cc2 = pe(4, lambda e: e.matmul(pb[4][:, 0:8], c_ones_row_f[:], r8_sb[:, 56:64],
                                         start=True, stop=True), wait=[t_cb])
        t_cbc = bread(4, "vector", lambda e, b=b: e.tensor_copy(
            comb_cols[:, b * NE:(b + 1) * NE], pb[4][:, 0:8]))
        for c in range(KD):
            t_ac = ev("vector", lambda e, c=c, b=b: e.tensor_scalar(
                out=b2c_cols[:, b * KD + c:b * KD + c + 1],
                in0=b2T[:, c * 8:c * 8 + 1],
                scalar1=comb_cols[:, b * NE:b * NE + 1], scalar2=None, op0=ALU.mult))
            for e2 in range(1, NE):
                t_ac = ev("vector", lambda e, c=c, b=b, e2=e2: e.scalar_tensor_tensor(
                    out=b2c_cols[:, b * KD + c:b * KD + c + 1],
                    in0=b2T[:, c * 8 + e2:c * 8 + e2 + 1],
                    scalar=comb_cols[:, b * NE + e2:b * NE + e2 + 1],
                    in1=b2c_cols[:, b * KD + c:b * KD + c + 1],
                    op0=ALU.mult, op1=ALU.add))

    if stop_after == "i2":
        return _finish()

    j_p2 = [p.last["tensor"], p.last["vector"], p.last["scalar"],
            (qA, qA.n), (qS, qS.n), (qY[0], qY[0].n), (qY[1], qY[1].n),
            (qZ[0], qZ[0].n), (qZ[1], qZ[1].n), (qX[0], qX[0].n), (qX[1], qX[1].n),
            (qWo, qWo.n)]
    g3 = {"w1": None, "w2": None, "ln2t": None, "ffn": None,
          "w1_rd": None, "w2_rd": None, "hm": None}
    MG = KF // 8
    rel_w1 = [None] * MG
    tok_w1g = [None] * MG
    for e_i in range(ne3):
        tok_w2 = None
        for mg in range(MG):
            tok_w1g[mg] = dma(
                "gpsimd",
                w1_bf[:, mg * 8192:(mg + 1) * 8192].rearrange("p (k f) -> p k f", k=KD),
                w1[e_i, :, mg * 1024:(mg + 1) * 1024].rearrange("(k p) f -> p k f", p=128),
                sem=qF,
                wait=(j_p2 if (e_i == 0 and mg == 0) else []) +
                     ([rel_w1[mg]] if rel_w1[mg] else []))
        for g in range(KF // 4):
            tok_w2 = dma("gpsimd", w2_bf[:, g * 4 * DIM:(g + 1) * 4 * DIM].rearrange(
                "p (k f) -> p k f", k=4),
                w2[e_i, g * 512:(g + 1) * 512, :].rearrange("(k p) f -> p k f", p=128),
                sem=qF, wait=[g3["w2_rd"]] if (g == 0 and g3["w2_rd"]) else [])
        for gt in range(nb * nt):
            b, ti = gt // NT, gt % NT
            t0 = ti * TILE
            tok_l = dma("sync", ln2t[:].rearrange("p (c t) -> p c t", c=KD),
                        ln2_sp[b, :, :, t0:t0 + TILE].rearrange("c p t -> p c t"),
                        sem=qL, wait=[g3["ln2t"]] if g3["ln2t"] else
                        [tok_ln2sp[-1], (qS, qS.n)])
            t_hm = [None] * KF
            t_mmA = None
            for m in range(KF):
                bk = m % 3
                mg, mw = m // 8, m % 8
                t_mmA = pe(bk, lambda e, mg=mg, mw=mw, bk=bk: [
                    e.matmul(pb[bk][:, 0:512],
                             w1_bf[:, mg * 8192 + k * 1024 + mw * 128:
                                   mg * 8192 + k * 1024 + (mw + 1) * 128],
                             ln2t[:, k * TILE:(k + 1) * TILE],
                             start=(k == 0), stop=(k == KD - 1)) for k in range(KD)][-1],
                    wait=([tok_l] if m == 0 else []) +
                         ([tok_w1g[mg]] if mw == 0 else []))
                t_hm[m] = bread(bk, "scalar", lambda e, m=m, bk=bk, e_i=e_i: e.activation(
                    out=hmid_bf[:, m * TILE:(m + 1) * TILE], in_=pb[bk][:, 0:512],
                    func=AF.Gelu, bias=b1_cols[:, e_i * KF + m:e_i * KF + m + 1]),
                    wait=[g3["hm"]] if (m == 0 and g3["hm"]) else [])
                if gt == nb * nt - 1 and mw == 7:
                    rel_w1[mg] = t_mmA
            g3["ln2t"] = t_mmA
            t_ffn = None
            t_mmB = None
            for o in range(KD):
                bk = 4 + o % 2
                t_mmB = pe(bk, lambda e, o=o, bk=bk: [
                    e.matmul(pb[bk][:, 0:512],
                             w2_bf[:, kf * DIM + o * 128:kf * DIM + (o + 1) * 128],
                             hmid_bf[:, kf * TILE:(kf + 1) * TILE],
                             start=(kf == 0), stop=(kf == KF - 1)) for kf in range(KF)][-1],
                    wait=[t_hm[KF - 1], tok_w2] if o == 0 else [])
                t_ffn = bread(bk, "vector", lambda e, o=o, bk=bk, b=b, e_i=e_i: e.tensor_scalar(
                    out=ffn_sb[:, o * TILE:(o + 1) * TILE], in0=pb[bk][:, 0:512],
                    scalar1=comb_cols[:, b * NE + e_i:b * NE + e_i + 1],
                    scalar2=None, op0=ALU.mult),
                    wait=[g3["ffn"]] if (o == 0 and g3["ffn"]) else [])
            g3["hm"] = t_mmB
            g3["w2_rd"] = t_mmB if gt == nb * nt - 1 else g3["w2_rd"]
            g3["ffn"] = dma("gpsimd", acc_sp[b, :, :, t0:t0 + TILE].rearrange("c p t -> p c t"),
                            ffn_sb.ap().rearrange("p (c t) -> p c t", c=KD),
                            sem=qAC, accum=(e_i > 0), wait=[t_ffn])

    if stop_after == "p3a":
        return _finish()

    j_p3 = [p.last["tensor"], p.last["vector"], p.last["scalar"],
            (qAC, qAC.n), (qL, qL.n), (qF, qF.n)]
    g4 = {"acc": None, "x2": None, "accrd": None, "x2rd": None, "out": [None, None]}
    tok_out = []
    for gt in range(nb * nt):
        b, ti = gt // NT, gt % NT
        t0 = ti * TILE
        t_la = dma("sync", acc_ld[:].rearrange("p (c t) -> p c t", c=KD),
                   acc_sp[b, :, :, t0:t0 + TILE].rearrange("c p t -> p c t"),
                   sem=qR1, wait=(j_p3 if gt == 0 else []) +
                                 ([g4["accrd"]] if g4["accrd"] else []))
        t_lx = dma("scalar", x2_ld[:].rearrange("p (c t) -> p c t", c=KD),
                   x2_sp[b, :, :, t0:t0 + TILE].rearrange("c p t -> p c t"),
                   sem=qR2, wait=(j_p3 if gt == 0 else []) +
                                 ([g4["x2rd"]] if g4["x2rd"] else []))
        t_b2a = None
        for o in range(KD):
            t_b2a = ev("vector", lambda e, o=o, b=b: e.tensor_scalar(
                out=acc_ld[:, o * TILE:(o + 1) * TILE], in0=acc_ld[:, o * TILE:(o + 1) * TILE],
                scalar1=b2c_cols[:, b * KD + o:b * KD + o + 1], scalar2=None, op0=ALU.add),
                wait=[t_la] if o == 0 else [])
        for s in range(NSUB):
            par = s % 2
            t_ta = None
            t_txx = None
            for o in range(KD):
                bk = 6 + (o // 4)
                slq = o % 4
                t_ta = pe(bk, lambda e, o=o, s=s, slq=slq, bk=bk: e.transpose(
                    pb[bk][:, slq * 128:(slq + 1) * 128],
                    acc_ld[:, o * TILE + s * 128:o * TILE + (s + 1) * 128], c_eye_f[:]),
                    wait=[t_b2a] if o == 0 else [])
            t_ea = [None, None]
            for half in range(2):
                t_ea[half] = bread(6 + half, "vector", lambda e, half=half: e.tensor_copy(
                    acc_tm[:, half * 512:(half + 1) * 512], pb[6 + half][:, 0:512]))
            for o in range(KD):
                bk = 2 + (o // 4)
                slq = o % 4
                t_txx = pe(bk, lambda e, o=o, s=s, slq=slq, bk=bk: e.transpose(
                    pb[bk][:, slq * 128:(slq + 1) * 128],
                    x2_ld[:, o * TILE + s * 128:o * TILE + (s + 1) * 128], c_eye_f[:]),
                    wait=[t_lx] if o == 0 else [])
            t_ex = [None, None]
            for half in range(2):
                t_ex[half] = bread(2 + half, "vector", lambda e, half=half: e.tensor_copy(
                    x2_tm[:, half * 512:(half + 1) * 512], pb[2 + half][:, 0:512]))
            t_rd = ev("vector", lambda e: e.reduce_sum(out=rtmp[:, 0:1], in_=acc_tm[:],
                                                       axis=AX.X))
            t_sqf = ev("scalar", lambda e: e.activation(
                out=trash3b[:], in_=acc_tm[:],
                func=AF.Square, accum_out=rtmp[:, 8:9]),
                wait=[t_ea[1]])
            ev("vector", lambda e: e.tensor_scalar(out=rtmp[:, 1:2], in0=rtmp[:, 0:1],
                                                   scalar1=1.0 / DIM, scalar2=None, op0=ALU.mult))
            ev("vector", lambda e: e.tensor_mul(rtmp[:, 2:3], rtmp[:, 1:2], rtmp[:, 1:2]))
            t_vr = ev("vector", lambda e: e.scalar_tensor_tensor(
                out=rtmp[:, 3:4], in0=rtmp[:, 8:9], scalar=1.0 / DIM, in1=rtmp[:, 2:3],
                op0=ALU.mult, op1=ALU.subtract), wait=[t_sqf])
            t_lnf = ev("scalar", lambda e: e.activation(out=rtmp[:, 9:10], in_=rtmp[:, 3:4],
                                                        func=AF.Ln, bias=cc(8)), wait=[t_vr])
            t_rf = ev("scalar", lambda e: e.activation(out=rtmp[:, 10:11], in_=rtmp[:, 9:10],
                                                       func=AF.Exp, scale=cc(3)))
            t_bf2 = ev("vector", lambda e: e.scalar_tensor_tensor(
                out=rtmp[:, 4:5], in0=rtmp[:, 1:2], scalar=-1.0, in1=rtmp[:, 10:11],
                op0=ALU.mult, op1=ALU.mult), wait=[t_rf])
            ov = out_tm[:, par * DIM:(par + 1) * DIM]
            t_ap2 = ev("vector", lambda e, ov=ov: e.tensor_scalar(
                out=ov, in0=acc_tm[:], scalar1=rtmp[:, 10:11], scalar2=rtmp[:, 4:5],
                op0=ALU.mult, op1=ALU.add),
                wait=[t_bf2] + ([g4["out"][par]] if g4["out"][par] else []))
            t_fin = ev("vector", lambda e, ov=ov: e.tensor_add(ov, ov, x2_tm[:]),
                       wait=[t_ex[1]])
            g4["out"][par] = dma("sync", out_t[b, t0 + s * 128:t0 + (s + 1) * 128, :], ov,
                                 sem=qO, wait=[t_fin])
            tok_out.append(g4["out"][par])
        g4["accrd"] = t_ta
        g4["x2rd"] = t_txx

    qall = [sems[k] for k in sems]
    p.emit("sync", None, wait=[(q, q.n) for q in qall if q.n > 0 and q.h.name.startswith("q")])

    nums = sorted(s.h.num for s in sems.values())
    assert nums == list(range(nums[0], nums[0] + len(nums))), nums
    srange = range(nums[0], nums[-1] + 1)
    if warmup:
        qD = nc.alloc_semaphore("qD")
        for _wd in range(warmup):
            nc.gpsimd.dma_start(out=acc_sp[1, _wd % KD, :, :],
                                in_=w1[_wd % NE, 128:256, 0:S]).then_inc(qD, 16)
        nc.gpsimd.wait_ge(qD, warmup * 16)
    nc.gpsimd.dma_reset(srange)
    nc.gpsimd.sem_clear(srange)
    nc.all_engine_barrier()
    with nc.Block() as block:
        p.replay(block)
    return nc


def make_aux():
    aux = {}
    aux["eye_bf"] = np.eye(128, dtype=BFNP)
    aux["eye_f"] = np.eye(128, dtype=np.float32)
    aux["triu_f"] = np.triu(np.ones((128, 128), np.float32))
    aux["ones_col_bf"] = np.ones((128, 1), BFNP)
    aux["ones_row_f"] = np.ones((1, 128), np.float32)
    selh = np.zeros((HEADS, HEADS * 128), np.float32)
    for h in range(HEADS):
        selh[h, h * 128:(h + 1) * 128] = 1.0
    aux["selh_f"] = selh
    selc = np.zeros((HEADS, 4 * 128), np.float32)
    for j in range(4):
        selc[2 * j, j * 128:j * 128 + 64] = 1.0
        selc[2 * j + 1, j * 128 + 64:j * 128 + 128] = 1.0
    aux["selc_f"] = selc
    pair16 = np.zeros((16, HEADS), np.float32)
    for h in range(HEADS):
        pair16[2 * h, h] = 1.0
        pair16[2 * h + 1, h] = 1.0
    aux["pair16_f"] = pair16
    dbl = np.zeros((HEADS, 16), np.float32)
    for k in range(HEADS):
        dbl[k, 2 * k] = 1.0
        dbl[k, 2 * k + 1] = 1.0
    aux["dblrow_f"] = dbl
    cvals = [-30.0, 1.0, -1.0, -0.5, 0.5, 1.0 / NORM_GN, 1.0 / DIM, 1.0 / S, EPS, 30.0]
    aux["consts_in"] = np.tile(np.array(cvals, np.float32), (128, 1))
    aux["zf_in"] = np.zeros((128, BC * 4 * HD), np.float32)
    aux["zb_in"] = np.zeros((128, BC * 4 * HD), BFNP)
    return aux


def make_in_maps(inputs, n_cores=8):
    x = np.ascontiguousarray(inputs["L0_attn_input_tensor"], dtype=np.float32)
    aux = make_aux()
    shared = dict(
        Wx=np.ascontiguousarray(np.asarray(inputs["L0_attn_Wx"]).T),
        Wz=np.ascontiguousarray(np.asarray(inputs["L0_attn_Wz"]).T),
        convw=np.ascontiguousarray(np.asarray(inputs["L0_attn_conv_w"]).reshape(KI, 128, KER)
                                   .transpose(1, 0, 2).reshape(128, KI * KER)),
        convb=np.ascontiguousarray(np.asarray(inputs["L0_attn_conv_b"]).reshape(KI, 128).T),
        Wb=np.ascontiguousarray(np.asarray(inputs["L0_attn_Wb"]).T),
        Wc=np.ascontiguousarray(np.asarray(inputs["L0_attn_Wc"]).T),
        Wdt=np.ascontiguousarray(np.asarray(inputs["L0_attn_Wdt"]).T),
        bdt=np.ascontiguousarray(np.asarray(inputs["L0_attn_b_dt"]).reshape(HEADS, 1)),
        loga=np.ascontiguousarray(np.asarray(inputs["L0_attn_log_a"]).reshape(HEADS, 1)),
        dpar=np.ascontiguousarray(np.asarray(inputs["L0_attn_d_param"]).reshape(HEADS, 1)),
        gnw=np.ascontiguousarray(np.asarray(inputs["L0_attn_gn_w"]).reshape(KI, 128).T),
        gnb=np.ascontiguousarray(np.asarray(inputs["L0_attn_gn_b"]).reshape(KI, 128).T),
        Wout=np.ascontiguousarray(np.asarray(inputs["L0_attn_Wout"]).T),
        gate_in=np.ascontiguousarray(np.asarray(inputs["L0_ffn_gate_w"]).reshape(KD, 128, NE)
                                     .transpose(1, 0, 2).reshape(128, KD * NE)),
        w1=np.ascontiguousarray(inputs["L0_ffn_w1"]),
        b1_in=np.ascontiguousarray(np.asarray(inputs["L0_ffn_b1"]).reshape(NE, KF, 128)
                                   .transpose(2, 0, 1).reshape(128, NE * KF)),
        w2=np.ascontiguousarray(inputs["L0_ffn_w2"]),
        b2t_in=np.ascontiguousarray(np.asarray(inputs["L0_ffn_b2"]).reshape(NE, KD, 128)
                                    .transpose(2, 1, 0).reshape(128, KD * NE)),
        **aux,
    )
    in_maps = []
    for c in range(n_cores):
        m = dict(shared)
        m["x"] = np.ascontiguousarray(x[BC * c:BC * (c + 1)])
        in_maps.append(m)
    return in_maps


def kernel(**inputs):
    nc = build()
    in_maps = make_in_maps(inputs)
    res = run_bass_kernel_spmd(nc, in_maps, core_ids=list(range(8)))
    return np.concatenate([np.asarray(r["out"], np.float32) for r in res.results], axis=0)


# version 03
# speedup vs baseline: 1.0408x; 1.0408x over previous
import numpy as np
import ml_dtypes
import concourse.bass as bass
import concourse.mybir as mybir
from concourse.bass_utils import run_bass_kernel_spmd

AF = mybir.ActivationFunctionType
ALU = mybir.AluOpType
AX = mybir.AxisListType
F32 = mybir.dt.float32
BF16 = mybir.dt.bfloat16
BFNP = ml_dtypes.bfloat16

DIM = 1024
INNER = 2048
HEADS = 8
STATE = 64
HD = 256
KER = 4
BC = 2
S = 2048
TILE = 512
NT = S // TILE
NSUB = TILE // 128
NCH = TILE // 128
KD = DIM // 128
KI = INNER // 128
FF = 4096
NE = 8
KF = FF // 128
EPS = 1e-5
NORM_GN = float(S * HD)

SB_BASE = 16512
SB_TOP = 229344
CW = KER - 1 + TILE


class Sem:
    __slots__ = ("h", "n")

    def __init__(self, h):
        self.h = h
        self.n = 0


class PB:
    ENGS = ("tensor", "vector", "scalar", "sync", "gpsimd")

    CHAIN = ("vector", "scalar", "gpsimd")

    def __init__(self, nc, strict=False):
        self.nc = nc
        self.strict = strict
        self.steps = {e: [] for e in self.ENGS}
        self.last = {e: None for e in self.ENGS}

    def emit(self, eng, fn, wait=(), inc=None, inc_n=1):
        w = tuple(t for t in wait if t is not None)
        if (self.strict or eng in self.CHAIN) and self.last[eng] is not None:
            w = w + (self.last[eng],)
        tok = None
        if inc is not None:
            inc.n += inc_n
            tok = (inc, inc.n)
            self.last[eng] = tok
        self.steps[eng].append((w, fn, (inc, inc_n) if inc is not None else None))
        return tok

    def replay(self, block):
        def mk(ename):
            def body(eng):
                for w, fn, inc in self.steps[ename]:
                    for s, v in w:
                        eng.wait_ge(s.h, v)
                    inst = fn(eng) if fn is not None else None
                    if inc is not None:
                        s, n = inc
                        if inst is None:
                            inst = eng.nop()
                        inst.then_inc(s.h, n)
            return body
        block.tensor(mk("tensor"))
        block.vector(mk("vector"))
        block.scalar(mk("scalar"))
        block.sync(mk("sync"))
        block.gpsimd(mk("gpsimd"))


LAST_MAP = {}


class SbufMap:
    def __init__(self, nc):
        self.nc = nc
        self.cur = SB_BASE
        self.marks = []
        self.hi = SB_BASE
        LAST_MAP.clear()

    def alloc(self, name, shape, dtype, align=128):
        per_part = int(np.prod(shape[1:])) * (4 if dtype == F32 else 2)
        self.cur = (self.cur + align - 1) // align * align
        assert self.cur + per_part <= SB_TOP, \
            f"SBUF overflow at {name}: {self.cur}+{per_part} > {SB_TOP}"
        h = self.nc.alloc_sbuf_tensor_at(name, list(shape), dtype, offset=self.cur)
        LAST_MAP[name] = (self.cur, tuple(shape), dtype)
        self.cur += per_part
        self.hi = max(self.hi, self.cur)
        return h

    def mark(self):
        self.marks.append(self.cur)

    def reset(self):
        self.cur = self.marks[-1]


def build(debug_taps=(), strict=False, scan_mask=7, barriers=0, delay_w=0, warmup=0, dup_w=0,
          nb=BC, nt=NT, stop_after=None, ne3=NE, mmv=7):
    nc = bass.Bass("TRN2", target_bir_lowering=False,
                   detect_race_conditions=bool(strict))
    DT = set(debug_taps)

    def inp(name, shape, dt=F32):
        return nc.dram_tensor(name, list(shape), dt, kind="ExternalInput").ap()

    x_in = inp("x", [BC, S, DIM])
    Wx = inp("Wx", [DIM, INNER])
    Wz = inp("Wz", [DIM, INNER])
    convw = inp("convw", [128, KI * KER])
    convb = inp("convb", [128, KI])
    Wb = inp("Wb", [INNER, HEADS * STATE])
    Wc = inp("Wc", [INNER, HEADS * STATE])
    Wdt = inp("Wdt", [INNER, HEADS])
    bdt = inp("bdt", [HEADS, 1])
    loga = inp("loga", [HEADS, 1])
    dpar = inp("dpar", [HEADS, 1])
    gnw = inp("gnw", [128, KI])
    gnb = inp("gnb", [128, KI])
    Wout = inp("Wout", [INNER, DIM])
    w1 = inp("w1", [NE, DIM, FF])
    w2 = inp("w2", [NE, FF, DIM])
    b2t_in = inp("b2t_in", [128, KD * NE])
    eye_bf = inp("eye_bf", [128, 128], BF16)
    eye_f = inp("eye_f", [128, 128])
    triu_f = inp("triu_f", [128, 128])
    ones_col_bf = inp("ones_col_bf", [128, 1], BF16)
    ones_row_f = inp("ones_row_f", [1, 128])
    selh_f = inp("selh_f", [HEADS, HEADS * 128])
    selc_f = inp("selc_f", [HEADS, 4 * 128])
    pair16_f = inp("pair16_f", [16, HEADS])
    dblrow_f = inp("dblrow_f", [HEADS, 16])
    consts_in = inp("consts_in", [128, 10])
    b1_in = inp("b1_in", [128, NE * KF])
    gate_in = inp("gate_in", [128, KD * NE])
    zf_in = inp("zf_in", [128, BC * 4 * HD])
    zb_in = inp("zb_in", [128, BC * 4 * HD], BF16)

    out_t = nc.dram_tensor("out", [BC, S, DIM], F32, kind="ExternalOutput").ap()

    def scratch(name, shape, dt):
        return nc.dram_tensor(name, list(shape), dt, kind="Internal").ap()

    w1b = scratch("w1b", [NE, DIM, FF], BF16)
    w2b = scratch("w2b", [NE, FF, DIM], BF16)
    y_sp = scratch("y_sp", [BC, KI, 128, S], BF16)
    sz_sp = scratch("sz_sp", [BC, KI, 128, S], BF16)
    x2_sp = scratch("x2_sp", [BC, KD, 128, S], F32)
    ln2_sp = scratch("ln2_sp", [BC, KD, 128, S], BF16)
    acc_sp = scratch("acc_sp", [BC, KD, 128, S], F32)

    taps = {}
    if "xc" in DT:
        taps["xc"] = nc.dram_tensor("tap_xc", [BC, KI, 128, S], BF16, kind="ExternalOutput").ap()
    if "bcdt" in DT:
        taps["bmat"] = nc.dram_tensor("tap_bmat", [BC, 4, 128, S], BF16, kind="ExternalOutput").ap()
        taps["dt"] = nc.dram_tensor("tap_dt", [BC, HEADS, S], F32, kind="ExternalOutput").ap()
    if "y" in DT:
        taps["y"] = nc.dram_tensor("tap_y", [BC, KI, 128, S], BF16, kind="ExternalOutput").ap()
    if "lnf" in DT:
        taps["lnf"] = nc.dram_tensor("tap_lnf", [BC, KD, 128, S], BF16, kind="ExternalOutput").ap()
    if "xsub" in DT:
        taps["xsub"] = nc.dram_tensor("tap_xsub", [NSUB, 128, DIM], F32, kind="ExternalOutput").ap()
        taps["stat"] = nc.dram_tensor("tap_stat", [NSUB, 128, 12], F32, kind="ExternalOutput").ap()
    if "gn" in DT:
        taps["gn"] = nc.dram_tensor("tap_gn", [128, 2 * BC * KI], F32, kind="ExternalOutput").ap()

    sm = SbufMap(nc)
    c_eye_bf = sm.alloc("c_eye_bf", [128, 128], BF16)
    c_eye_f = sm.alloc("c_eye_f", [128, 128], F32)
    c_triu = sm.alloc("c_triu", [128, 128], F32)
    c_ones_col_bf = sm.alloc("c_ones_col_bf", [128, 1], BF16)
    c_ones_row_f = sm.alloc("c_ones_row_f", [1, 128], F32)
    c_selh = sm.alloc("c_selh", [HEADS, HEADS * 128], F32)
    c_selc = sm.alloc("c_selc", [HEADS, 4 * 128], F32)
    c_pair16 = sm.alloc("c_pair16", [16, HEADS], F32)
    c_dblrow = sm.alloc("c_dblrow", [HEADS, 16], F32)
    c_consts = sm.alloc("c_consts", [128, 10], F32)
    convw_cols = sm.alloc("convw_cols", [128, KI * KER], F32)
    convb_col = sm.alloc("convb_col", [128, KI], F32)
    gnw_col = sm.alloc("gnw_col", [128, KI], F32)
    gnb_col = sm.alloc("gnb_col", [128, KI], F32)
    bdt_col = sm.alloc("bdt_col", [HEADS, 1], F32)
    loga_col = sm.alloc("loga_col", [HEADS, 1], F32)
    a_col = sm.alloc("a_col", [HEADS, 1], F32)
    dpar_col = sm.alloc("dpar_col", [HEADS, 1], F32)
    dT = sm.alloc("dT", [128, HEADS], F32)
    junk8 = sm.alloc("junk8", [HEADS, 128], F32)
    zeros8 = sm.alloc("zeros8", [HEADS, 128], F32)
    stats_sum = sm.alloc("stats_sum", [128, BC * KI], F32)
    stats_sq = sm.alloc("stats_sq", [128, BC * KI], F32)
    gn_scale = sm.alloc("gn_scale", [128, BC * KI], F32)
    gn_bias = sm.alloc("gn_bias", [128, BC * KI], F32)
    racc = sm.alloc("racc", [128, BC * KD], F32)
    comb_cols = sm.alloc("comb_cols", [128, BC * NE], F32)
    b2c_cols = sm.alloc("b2c_cols", [128, BC * KD], F32)
    rtmp = sm.alloc("rtmp", [128, 64], F32)
    rtmp8 = sm.alloc("rtmp8", [HEADS, 160], F32)
    b1_cols = sm.alloc("b1_cols", [128, NE * KF], F32)
    gate_sb = sm.alloc("gate_sb", [128, KD * NE], F32)
    b2T = sm.alloc("b2T", [128, KD * NE], F32)

    sm.mark()
    wxT = [sm.alloc(f"wxT{k}", [128, INNER], BF16) for k in range(KD)]
    wzT = [sm.alloc(f"wzT{k}", [128, INNER], BF16) for k in range(KD)]
    wbT = [sm.alloc(f"wbT{k}", [128, HEADS * STATE], BF16) for k in range(KI)]
    wcT = [sm.alloc(f"wcT{k}", [128, HEADS * STATE], BF16) for k in range(KI)]
    wdtT = [sm.alloc(f"wdtT{k}", [128, HEADS], BF16) for k in range(KI)]
    st_f32 = sm.alloc("st_f32", [128, BC * 4 * HD], F32)
    st_bf = sm.alloc("st_bf", [128, BC * 4 * HD], BF16)
    x_tm = sm.alloc("x_tm", [128, 2 * DIM], F32)
    ln1_sub = sm.alloc("ln1_sub", [128, 2 * DIM], BF16)
    ln1_fm = sm.alloc("ln1_fm", [128, KD * TILE], BF16)
    ln1_cols = sm.alloc("ln1_cols", [128, 2 * NSUB], F32)
    xpc = sm.alloc("xpc", [128, 2 * CW], BF16)
    xtail = sm.alloc("xtail", [128, KI * (KER - 1)], BF16)
    xc_bf = sm.alloc("xc_bf", [128, KI * TILE], BF16)
    cvt = sm.alloc("cvt", [128, 2 * TILE], F32)
    sz_bf = sm.alloc("sz_bf", [128, 4 * TILE], BF16)
    bmat = sm.alloc("bmat", [128, 4 * TILE], BF16)
    cmat = sm.alloc("cmat", [128, 4 * TILE], BF16)
    ctil = sm.alloc("ctil", [128, 4 * TILE], BF16)
    dt_fm = sm.alloc("dt_fm", [HEADS, TILE], F32)
    dt_sc = sm.alloc("dt_sc", [HEADS, 3 * TILE], F32)
    alpha = sm.alloc("alpha", [HEADS, TILE], F32)
    alpham1 = sm.alloc("alpham1", [HEADS, 128], F32)
    erow = sm.alloc("erow", [HEADS, 128], F32)
    wexp8 = sm.alloc("wexp8", [HEADS, 128], F32)
    decb8 = sm.alloc("decb8", [HEADS, 128], F32)
    scn_sm = sm.alloc("scn_sm", [128, 2 * 32], F32)
    mt_bf = sm.alloc("mt_bf", [128, 2 * 128], BF16)
    e_f32 = sm.alloc("e_f32", [128, 2 * 128], F32)
    xtm_bf = sm.alloc("xtm_bf", [128, 2 * HD], BF16)
    bw_bf = sm.alloc("bw_bf", [128, 2 * STATE], BF16)
    y_sb = sm.alloc("y_sb", [128, KI * 128], BF16)
    sq_bf = sm.alloc("sq_bf", [128, KI * 128], BF16)
    sm.reset()
    woutT = sm.alloc("woutT", [128, KI * DIM], BF16)
    y_ld = sm.alloc("y_ld", [128, 2 * 4 * TILE], BF16)
    sz_ld = sm.alloc("sz_ld", [128, 2 * 4 * TILE], BF16)
    x_tm2 = sm.alloc("x_tm2", [128, 2 * DIM], F32)
    x_fm = sm.alloc("x_fm", [128, KD * TILE], F32)
    u_bf = sm.alloc("u_bf", [128, KI * TILE], BF16)
    gnt_bf = sm.alloc("gnt_bf", [128, TILE], BF16)
    attn_f = sm.alloc("attn_f", [128, KD * TILE], F32)
    attn_bf = sm.alloc("attn_bf", [128, KD * TILE], BF16)
    sq2_bf = sm.alloc("sq2_bf", [128, KD * TILE], BF16)
    x2_f = sm.alloc("x2_f", [128, KD * TILE], F32)
    ln2_bf = sm.alloc("ln2_bf", [128, KD * TILE], BF16)
    rows_sb = sm.alloc("rows_sb", [1, 4 * TILE], F32)
    r8_sb = sm.alloc("r8_sb", [1, 64], F32)
    sm.reset()
    w1_bf = sm.alloc("w1_bf", [128, KD * FF], BF16)
    w2_bf = sm.alloc("w2_bf", [128, KF * DIM], BF16)
    hmid_bf = sm.alloc("hmid_bf", [128, KF * TILE], BF16)
    ln2t = sm.alloc("ln2t", [128, KD * TILE], BF16)
    ffn_sb = sm.alloc("ffn_sb", [128, KD * TILE], F32)
    sm.reset()
    acc_ld = sm.alloc("acc_ld", [128, KD * TILE], F32)
    x2_ld = sm.alloc("x2_ld", [128, KD * TILE], F32)
    acc_tm = sm.alloc("acc_tm", [128, DIM], F32)
    x2_tm = sm.alloc("x2_tm", [128, DIM], F32)
    out_tm = sm.alloc("out_tm", [128, 2 * DIM], F32)
    trash3b = sm.alloc("trash3b", [128, DIM], BF16)
    sm.reset()

    pb = [nc.alloc_psum_tensor(f"pb{i}", [128, 512], F32) for i in range(8)]
    pbbf = [h.ap().bitcast(BF16) for h in pb]

    sems = {n: Sem(nc.alloc_semaphore(n)) for n in
            ("qS", "qA", "qG", "qW", "qV", "qX0", "qX1", "qT", "tS", "vS", "sS", "gS", "yS",
             "qWo", "qY0", "qY1", "qZ0", "qZ1", "qL", "qF", "qAC", "qR1", "qR2", "qO",
             "qWB")}
    qS, qA, qG, qW, qV = (sems["qS"], sems["qA"], sems["qG"], sems["qW"], sems["qV"])
    qX = [sems["qX0"], sems["qX1"]]
    qT = sems["qT"]
    tS, vS, sS, gS = sems["tS"], sems["vS"], sems["sS"], sems["gS"]
    QSEM = {"sync": qS, "scalar": qA, "gpsimd": qG}
    ESEM = {"tensor": tS, "vector": vS, "scalar": sS, "gpsimd": gS, "sync": sems["yS"]}

    p = PB(nc, strict=strict)

    def dma(queue, dst, src, wait=(), accum=False, slow=False, transpose=False, sem=None):
        if queue == "sync#V":
            queue, sem = "sync", qV
        q = sem if sem is not None else QSEM[queue]
        kw = {}
        if accum:
            kw["accum_op"] = ALU.add
        if slow:
            kw["allow_slow_non_contiguous"] = True
        if transpose:
            return p.emit(queue, lambda e, dst=dst, src=src: e.dma_start_transpose(dst, src),
                          wait=wait, inc=q, inc_n=16)
        return p.emit(queue, lambda e, dst=dst, src=src, kw=kw: e.dma_start(out=dst, in_=src, **kw),
                      wait=wait, inc=q, inc_n=16)

    def cc(j, n=128):
        return c_consts[0:n, j:j + 1]

    bank_reads = [[] for _ in range(8)]
    bank_pe = [None] * 8

    def pe(i, fn, wait=()):
        w = list(wait) + bank_reads[i]
        bank_reads[i] = []
        tok = p.emit("tensor", fn, wait=w, inc=tS)
        bank_pe[i] = tok
        return tok

    def bread(i, eng, fn, wait=()):
        w = list(wait) + ([bank_pe[i]] if bank_pe[i] is not None else [])
        tok = p.emit(eng, fn, wait=w, inc=ESEM[eng])
        bank_reads[i].append(tok)
        return tok

    def ev(eng, fn, wait=()):
        return p.emit(eng, fn, wait=wait, inc=ESEM[eng])

    ALL_Q = [sems[k] for k in ("qS", "qA", "qG", "qW", "qV", "qX0", "qX1", "qT")]

    def _finish():
        qall_ = [sems[k] for k in sems]
        p.emit("sync", None,
               wait=[(q, q.n) for q in qall_ if q.n > 0 and q.h.name.startswith("q")])
        nums_ = sorted(s.h.num for s in sems.values())
        srange_ = range(nums_[0], nums_[-1] + 1)
        nc.gpsimd.dma_reset(srange_)
        nc.gpsimd.sem_clear(srange_)
        nc.all_engine_barrier()
        with nc.Block() as block_:
            p.replay(block_)
        return nc

    def barrier(level):
        if not (barriers & level):
            return
        toks = [p.emit(e, None, inc=ESEM[e]) for e in PB.ENGS]
        targets = toks + [(q, q.n) for q in ALL_Q if q.n > 0]
        for e in PB.ENGS:
            p.emit(e, None, wait=targets)

    ld = {}
    ld["eye_bf"] = dma("sync#V", c_eye_bf[:], eye_bf, sem=qW)
    ld["eye_f"] = dma("sync#V", c_eye_f[:], eye_f, sem=qW)
    ld["triu"] = dma("sync#V", c_triu[:], triu_f, sem=qW)
    ld["ones_bf"] = dma("sync#V", c_ones_col_bf[:], ones_col_bf, sem=qW)
    ld["ones_f"] = dma("sync#V", c_ones_row_f[:], ones_row_f, sem=qW)
    ld["selh"] = dma("sync#V", c_selh[:], selh_f, sem=qW)
    ld["selc"] = dma("sync#V", c_selc[:], selc_f, sem=qW)
    ld["pair16"] = dma("sync#V", c_pair16[:], pair16_f, sem=qW)
    ld["dblrow"] = dma("sync#V", c_dblrow[:], dblrow_f, sem=qW)
    ld["convw"] = dma("sync", convw_cols[:], convw, sem=qV)
    ld["convb"] = dma("sync", convb_col[:], convb, sem=qV)
    ld["gnw"] = dma("sync", gnw_col[:], gnw, sem=qV)
    ld["gnb"] = dma("sync", gnb_col[:], gnb, sem=qV)
    ld["bdt"] = dma("sync", bdt_col[:], bdt, sem=qV)
    ld["loga"] = dma("sync", loga_col[:], loga, sem=qV)
    ld["dpar"] = dma("sync", dpar_col[:], dpar, sem=qV)
    for _d in range(delay_w):
        dma("gpsimd", acc_sp[0, _d % KD, :, :], w1[_d % NE, 0:128, 0:S], sem=qG)
    tok_wxT = [dma("gpsimd", wxT[k][:], Wx[k * 128:(k + 1) * 128, :], sem=qW) for k in range(KD)]
    tok_wzT = [dma("gpsimd", wzT[k][:], Wz[k * 128:(k + 1) * 128, :], sem=qW) for k in range(KD)]
    tok_wbT = [dma("gpsimd", wbT[k][:], Wb[k * 128:(k + 1) * 128, :], sem=qW) for k in range(KI)]
    tok_wcT = [dma("gpsimd", wcT[k][:], Wc[k * 128:(k + 1) * 128, :], sem=qW) for k in range(KI)]
    tok_wdtT = [dma("gpsimd", wdtT[k][:], Wdt[k * 128:(k + 1) * 128, :], sem=qW) for k in range(KI)]
    qWB = sems["qWB"]
    for e2 in range(NE):
        for k in range(KD):
            dma("gpsimd", w1b[e2, k * 128:(k + 1) * 128, :],
                w1[e2, k * 128:(k + 1) * 128, :], sem=qWB)
        for g2 in range(KD):
            dma("gpsimd", w2b[e2, g2 * 512:(g2 + 1) * 512, :],
                w2[e2, g2 * 512:(g2 + 1) * 512, :], sem=qWB)
    tok_wb = (qWB, qWB.n)
    for _r in range(dup_w):
        for k in range(KD):
            dma("gpsimd", wxT[k][:], Wx[k * 128:(k + 1) * 128, :], sem=qW)
            dma("gpsimd", wzT[k][:], Wz[k * 128:(k + 1) * 128, :], sem=qW)
        for k in range(KI):
            dma("gpsimd", wbT[k][:], Wb[k * 128:(k + 1) * 128, :], sem=qW)
            dma("gpsimd", wcT[k][:], Wc[k * 128:(k + 1) * 128, :], sem=qW)
            dma("gpsimd", wdtT[k][:], Wdt[k * 128:(k + 1) * 128, :], sem=qW)
    qW_all = (qW, qW.n)

    dma("sync", c_consts[:], consts_in, sem=qV)
    dma("sync", zeros8[:], zf_in[0:HEADS, 0:128], sem=qV)
    dma("sync", junk8[:], zf_in[0:HEADS, 0:128], sem=qV)
    dma("sync", st_f32[:], zf_in, sem=qV)
    dma("sync", st_bf[:], zb_in, sem=qV)
    dma("sync", stats_sum[:], zf_in[:, 0:BC * KI], sem=qV)
    dma("sync", stats_sq[:], zf_in[:, 0:BC * KI], sem=qV)
    dma("sync", racc[:], zf_in[:, 0:BC * KD], sem=qV)
    dma("sync", b1_cols[:], b1_in, sem=qV)
    dma("sync", gate_sb[:], gate_in, sem=qV)
    dma("sync", b2T[:], b2t_in, sem=qV)
    qV_all = (qV, qV.n)
    tok_consts = ev("vector", lambda e: e.nop(), wait=[qW_all, qV_all])

    tok_a = ev("scalar", lambda e: e.activation(out=a_col[:], in_=loga_col[:], func=AF.Exp),
               wait=[qW_all, tok_consts])
    tok_a = ev("vector", lambda e: e.tensor_scalar(out=a_col[:], in0=a_col[:],
                                                   scalar1=-1.0, scalar2=None, op0=ALU.mult),
               wait=[tok_a])
    tok_dbc = ev("vector", lambda e: e.tensor_scalar(out=junk8[:], in0=junk8[:],
                                                     scalar1=0.0, scalar2=dpar_col[:],
                                                     op0=ALU.mult, op1=ALU.add),
                 wait=[qW_all, tok_consts])
    tok_dtr = pe(2, lambda e: e.transpose(pb[2][:, 0:8], junk8[:], c_eye_f[0:8, 0:8]),
                 wait=[tok_dbc, qV_all])
    tok_dT = bread(2, "vector", lambda e: e.tensor_copy(dT[:], pb[2][:, 0:8]))
    tok_junk0 = dma("sync", junk8[:], zf_in[0:HEADS, 0:128], wait=[tok_dtr], sem=qT)

    gates = {
        "x_tm": [None, None],
        "rtmpA": [None, None],
        "ln1_sub": [None, None],
        "ln1_fm": None,
        "xpc": [None, None],
        "xc_t": None, "xc_v": None,
        "bc": None,
        "ctil": None,
        "aline": None,
        "cvt": [None, None],
        "sz": None,
        "y": None,
        "ysq": None,
        "sq": None,
        "erow": None, "wexp": None, "am1": None, "decb": None,
        "state": [tok_consts] * (BC * HEADS),
        "mt": [None, None],
        "xtm": [None, None],
        "bw": [None, None],
        "ef": [None, None],
    }
    tap_toks = []

    def _flat(ts):
        out = []
        for t in ts:
            if t is None:
                continue
            if isinstance(t, tuple) and len(t) == 2 and isinstance(t[0], Sem):
                out.append(t)
            elif isinstance(t, tuple):
                out.extend(x for x in t if x is not None)
            else:
                out.append(t)
        return out

    for b in range(nb):
        for ti in range(nt):
            t0 = ti * TILE
            for s in range(NSUB):
                par = s % 2
                xv = x_tm[:, par * DIM:(par + 1) * DIM]
                t_ld = dma("sync", xv, x_in[b, t0 + s * 128:t0 + (s + 1) * 128, :],
                           sem=qX[par], wait=[gates["x_tm"][par]] if gates["x_tm"][par] else [])
                rb = 32 * par
                rs = 32 * par + 8
                t_red = ev("vector", lambda e, xv=xv, rb=rb: e.reduce_sum(
                    out=rtmp[:, rb:rb + 1], in_=xv, axis=AX.X),
                    wait=[t_ld, qW_all, qV_all])
                t_sq = ev("scalar", lambda e, xv=xv, rs=rs: e.activation(
                    out=sq_bf[:, 0:DIM], in_=xv, func=AF.Square,
                    accum_out=rtmp[:, rs:rs + 1]),
                    wait=[t_ld, tok_consts] +
                         [t for t in (gates["rtmpA"][par], gates["sq"]) if t])
                t_mu = ev("vector", lambda e, rb=rb: e.tensor_scalar(
                    out=rtmp[:, rb + 1:rb + 2], in0=rtmp[:, rb:rb + 1],
                    scalar1=1.0 / DIM, scalar2=None, op0=ALU.mult), wait=[t_red])
                t_m2 = ev("vector", lambda e, rb=rb: e.tensor_mul(
                    rtmp[:, rb + 2:rb + 3], rtmp[:, rb + 1:rb + 2], rtmp[:, rb + 1:rb + 2]),
                    wait=[])
                t_var = ev("vector", lambda e, rb=rb, rs=rs: e.scalar_tensor_tensor(
                    out=rtmp[:, rb + 3:rb + 4], in0=rtmp[:, rs:rs + 1],
                    scalar=1.0 / DIM, in1=rtmp[:, rb + 2:rb + 3],
                    op0=ALU.mult, op1=ALU.subtract), wait=[t_sq])
                gates["rtmpA"][par] = t_var
                t_lnv = ev("scalar", lambda e, rb=rb, rs=rs: e.activation(
                    out=rtmp[:, rs + 1:rs + 2], in_=rtmp[:, rb + 3:rb + 4],
                    func=AF.Ln, bias=cc(8)), wait=[t_var, tok_consts])
                t_r = ev("scalar", lambda e, rs=rs, s=s: e.activation(
                    out=ln1_cols[:, 2 * s:2 * s + 1], in_=rtmp[:, rs + 1:rs + 2],
                    func=AF.Exp, scale=cc(3)), wait=[tok_consts])
                t_B = ev("vector", lambda e, rb=rb, s=s: e.scalar_tensor_tensor(
                    out=ln1_cols[:, 2 * s + 1:2 * s + 2], in0=rtmp[:, rb + 1:rb + 2],
                    scalar=-1.0, in1=ln1_cols[:, 2 * s:2 * s + 1],
                    op0=ALU.mult, op1=ALU.mult), wait=[t_r])
                lv = ln1_sub[:, par * DIM:(par + 1) * DIM]
                if "xsub" in DT and b == 0 and ti == 0:
                    tap_toks.append(dma("sync", taps["xsub"][s], xv, wait=[t_red], sem=qT))
                    tap_toks.append(dma("sync", taps["stat"][s, :, 0:10],
                                        rtmp[:, rb:rb + 10], wait=[t_B], sem=qT))
                t_ap = ev("scalar", lambda e, xv=xv, lv=lv, s=s: e.activation(
                    out=lv, in_=xv, func=AF.Identity,
                    scale=ln1_cols[:, 2 * s:2 * s + 1], bias=ln1_cols[:, 2 * s + 1:2 * s + 2]),
                    wait=[t_B] + ([gates["ln1_sub"][par]] if gates["ln1_sub"][par] else []))
                gates["x_tm"][par] = t_ap
                t_tr = None
                for k in range(KD):
                    bk = 2 + (k % 2)
                    sl = k // 2
                    t_tr = pe(bk, lambda e, lv=lv, k=k, sl=sl, bk=bk: e.transpose(
                        pbbf[bk][:, sl * 128:(sl + 1) * 128],
                        lv[:, k * 128:(k + 1) * 128], c_eye_bf[:]),
                        wait=[t_ap])
                gates["ln1_sub"][par] = t_tr
                lfv = ln1_fm.ap().rearrange("p (k4 two t) -> p k4 two t", k4=4, two=2)
                for par2 in range(2):
                    tok_fm = bread(2 + par2, "vector",
                                   lambda e, s=s, par2=par2, lfv=lfv: e.tensor_copy(
                                       lfv[:, :, par2, s * 128:(s + 1) * 128],
                                       pbbf[2 + par2][:, 0:512].rearrange(
                                           "p (k t) -> p k t", k=4)),
                                   wait=[gates["ln1_fm"]] if (s == 0 and par2 == 0 and gates["ln1_fm"]) else [])
            barrier(1)
            if "lnf" in DT:
                tap_toks.append(dma("sync",
                                    taps["lnf"][b, :, :, t0:t0 + TILE].rearrange("c p t -> p c t"),
                                    ln1_fm.ap().rearrange("p (c t) -> p c t", c=KD),
                                    wait=[tok_fm], sem=qT))
            tok_conv_last = None
            for m in range(KI):
                bk = m % 2
                par = m % 2
                t_mmx = pe(bk, lambda e, m=m, bk=bk: [
                    e.matmul(pb[bk][:, 0:512], wxT[k][:, m * 128:(m + 1) * 128],
                             ln1_fm[:, k * TILE:(k + 1) * TILE],
                             start=(k == 0), stop=(k == KD - 1)) for k in range(KD)][-1],
                    wait=[tok_fm, qW_all, qV_all] if m == 0 else [])
                xv = xpc[:, par * CW:(par + 1) * CW]
                t_xpc = bread(bk, "scalar", lambda e, xv=xv, bk=bk: e.activation(
                    out=xv[:, KER - 1:CW], in_=pb[bk][:, 0:512], func=AF.Copy),
                    wait=[gates["xpc"][par]] if gates["xpc"][par] else [])
                if ti == 0:
                    t_tf = ev("vector", lambda e, xv=xv: e.tensor_scalar(
                        out=xv[:, 0:KER - 1], in0=convb_col[:, 0:KER - 1],
                        scalar1=0.0, scalar2=None, op0=ALU.mult),
                              wait=[t_xpc, tok_consts])
                else:
                    t_tf = ev("vector", lambda e, xv=xv, m=m: e.tensor_copy(
                        xv[:, 0:KER - 1], xtail[:, m * (KER - 1):(m + 1) * (KER - 1)]),
                        wait=[t_xpc])
                cv = cvt[:, par * TILE:(par + 1) * TILE]
                t1 = ev("vector", lambda e, xv=xv, m=m, cv=cv: e.tensor_scalar(
                    out=cv, in0=xv[:, 0:TILE],
                    scalar1=convw_cols[:, m * KER:m * KER + 1], scalar2=None, op0=ALU.mult),
                    wait=[t_tf] + ([gates["cvt"][par]] if gates["cvt"][par] else []))
                for k in range(1, KER):
                    t1 = ev("vector", lambda e, xv=xv, m=m, k=k, cv=cv: e.scalar_tensor_tensor(
                        out=cv, in0=xv[:, k:k + TILE],
                        scalar=convw_cols[:, m * KER + k:m * KER + k + 1], in1=cv,
                        op0=ALU.mult, op1=ALU.add), wait=[])
                t_tl = ev("vector", lambda e, xv=xv, m=m: e.tensor_copy(
                    xtail[:, m * (KER - 1):(m + 1) * (KER - 1)], xv[:, TILE:TILE + KER - 1]),
                    wait=[])
                gates["xpc"][par] = t_tl
                wsil = [t1]
                if m == 0:
                    wsil += [t for t in (gates["xc_t"], gates["xc_v"]) if t]
                tok_sil = ev("scalar", lambda e, m=m, cv=cv: e.activation(
                    out=xc_bf[:, m * TILE:(m + 1) * TILE], in_=cv, func=AF.Silu,
                    bias=convb_col[:, m:m + 1]), wait=wsil)
                gates["cvt"][par] = tok_sil
                tok_conv_last = tok_sil
            for m in range(KI):
                bk = m % 2
                t_mmz = pe(bk, lambda e, m=m, bk=bk: [
                    e.matmul(pb[bk][:, 0:512], wzT[k][:, m * 128:(m + 1) * 128],
                             ln1_fm[:, k * TILE:(k + 1) * TILE],
                             start=(k == 0), stop=(k == KD - 1)) for k in range(KD)][-1])
                tok_sz = bread(bk, "scalar", lambda e, m=m, bk=bk: e.activation(
                    out=sz_bf[:, (m % 4) * TILE:(m % 4 + 1) * TILE],
                    in_=pb[bk][:, 0:512], func=AF.Silu),
                    wait=[gates["sz"]] if (m % 4 == 0 and gates["sz"]) else [])
                if m % 4 == 3:
                    g0 = (m // 4) * 4
                    gates["sz"] = dma(
                        "scalar", sz_sp[b, g0:g0 + 4, :, t0:t0 + TILE].rearrange("c p t -> p c t"),
                        sz_bf.ap().rearrange("p (c t) -> p c t", c=4),
                        wait=[tok_sz])
            gates["ln1_fm"] = t_mmz
            if "xc" in DT:
                tap_toks.append(dma("sync",
                                    taps["xc"][b, :, :, t0:t0 + TILE].rearrange("c p t -> p c t"),
                                    xc_bf.ap().rearrange("p (c t) -> p c t", c=KI),
                                    wait=[tok_conv_last], sem=qT))
            barrier(2)
            tok_bm = [None] * 4
            tok_cm = [None] * 4
            for j in range(4):
                bk = j % 2
                t_mmb = pe(bk, lambda e, j=j, bk=bk: [
                    e.matmul(pb[bk][:, 0:512], wbT[k][:, j * 128:(j + 1) * 128],
                             xc_bf[:, k * TILE:(k + 1) * TILE],
                             start=(k == 0), stop=(k == KI - 1)) for k in range(KI)][-1],
                    wait=([tok_conv_last, qW_all] if j == 0 else []))
                tok_bm[j] = bread(bk, "vector", lambda e, j=j, bk=bk: e.tensor_copy(
                    bmat[:, j * TILE:(j + 1) * TILE], pb[bk][:, 0:512]),
                    wait=[gates["bc"]] if (j == 0 and gates["bc"]) else [])
            for j in range(4):
                bk = j % 2
                t_mmc = pe(bk, lambda e, j=j, bk=bk: [
                    e.matmul(pb[bk][:, 0:512], wcT[k][:, j * 128:(j + 1) * 128],
                             xc_bf[:, k * TILE:(k + 1) * TILE],
                             start=(k == 0), stop=(k == KI - 1)) for k in range(KI)][-1])
                tok_cm[j] = bread(bk, "vector", lambda e, j=j, bk=bk: e.tensor_copy(
                    cmat[:, j * TILE:(j + 1) * TILE], pb[bk][:, 0:512]))
            t_mmd = pe(4, lambda e: [
                e.matmul(pb[4][0:HEADS, 0:512], wdtT[k][:, 0:HEADS],
                         xc_bf[:, k * TILE:(k + 1) * TILE],
                         start=(k == 0), stop=(k == KI - 1)) for k in range(KI)][-1])
            t_x = bread(4, "vector", lambda e: e.tensor_scalar(
                out=dt_sc[:, 0:TILE], in0=pb[4][0:HEADS, 0:512], scalar1=bdt_col[:],
                scalar2=None, op0=ALU.add),
                wait=[qW_all] + ([gates["aline"]] if gates["aline"] else []))
            t_ab = ev("scalar", lambda e: e.activation(
                out=dt_sc[:, TILE:2 * TILE], in_=dt_sc[:, 0:TILE],
                func=AF.Abs), wait=[t_x])
            t_e = ev("scalar", lambda e: e.activation(
                out=dt_sc[:, TILE:2 * TILE], in_=dt_sc[:, TILE:2 * TILE],
                func=AF.Exp, scale=cc(2, HEADS)), wait=[])
            t_l = ev("scalar", lambda e: e.activation(
                out=dt_sc[:, 2 * TILE:3 * TILE], in_=dt_sc[:, TILE:2 * TILE],
                func=AF.Ln, bias=cc(1, HEADS)), wait=[tok_consts])
            t_rl = ev("vector", lambda e: e.tensor_single_scalar(
                out=dt_sc[:, TILE:2 * TILE], in_=dt_sc[:, 0:TILE], scalar=0.0,
                op=ALU.max), wait=[t_l])
            tok_dt = ev("vector", lambda e: e.tensor_add(
                dt_fm[:], dt_sc[:, TILE:2 * TILE], dt_sc[:, 2 * TILE:3 * TILE]),
                wait=[])
            if "bcdt" in DT:
                tap_toks.append(dma("sync",
                                    taps["bmat"][b, :, :, t0:t0 + TILE].rearrange("c p t -> p c t"),
                                    bmat.ap().rearrange("p (c t) -> p c t", c=4),
                                    wait=[tok_bm[3]], sem=qT))
                tap_toks.append(dma("sync", taps["dt"][b, :, t0:t0 + TILE], dt_fm[:],
                                    wait=[tok_dt], sem=qT))
            barrier(4)
            t_ye = None
            for c in range(NCH):
                barrier(8)
                cs = c * 128
                cp = c % 2
                smb = cp * 32
                bsc = 2 + cp
                bet = 4 + cp
                t_dta = ev("vector", lambda e, cs=cs: e.tensor_scalar(
                    out=rtmp8[:, 0:128], in0=dt_fm[:, cs:cs + 128], scalar1=a_col[:],
                    scalar2=None, op0=ALU.mult),
                    wait=[tok_dt, tok_a] + ([gates["aline"]] if (c == 0 and gates["aline"]) else []))
                t_al = ev("vector", lambda e, cs=cs: e.tensor_tensor_scan(
                    out=alpha[:, cs:cs + 128], data0=rtmp8[:, 0:128], data1=zeros8[:],
                    initial=0.0, op0=ALU.add, op1=ALU.add), wait=[tok_consts])
                t_er = ev("scalar", lambda e, cs=cs: e.activation(
                    out=erow[:], in_=alpha[:, cs:cs + 128], func=AF.Exp),
                    wait=[t_al] + ([gates["erow"]] if gates["erow"] else []))
                t_am1 = ev("vector", lambda e, cs=cs: e.tensor_scalar(
                    out=alpham1[:], in0=alpha[:, cs:cs + 128],
                    scalar1=alpha[:, cs + 127:cs + 128], scalar2=None, op0=ALU.subtract),
                    wait=[gates["am1"]] if gates["am1"] else [])
                t_wex = ev("scalar", lambda e: e.activation(
                    out=wexp8[:], in_=alpham1[:], func=AF.Exp, scale=cc(2, HEADS)),
                    wait=[t_am1, tok_consts] + ([gates["wexp"]] if gates["wexp"] else []))
                t_dec = ev("scalar", lambda e, cs=cs: e.activation(
                    out=rtmp8[:, 128:129], in_=alpha[:, cs + 127:cs + 128], func=AF.Exp),
                    wait=[gates["decb"]] if gates["decb"] else [])
                t_decb = ev("vector", lambda e: e.tensor_scalar(
                    out=decb8[:], in0=junk8[:], scalar1=0.0,
                    scalar2=rtmp8[:, 128:129], op0=ALU.mult, op1=ALU.add),
                    wait=[t_dec, tok_junk0])
                gates["decb"] = t_decb
                t_t1 = pe(bsc, lambda e, cs=cs, bsc=bsc: e.transpose(
                    pb[bsc][:, 0:8], alpha[:, cs:cs + 128], c_eye_f[0:8, 0:8]), wait=[t_al])
                t_t2 = pe(bsc, lambda e, bsc=bsc: e.transpose(
                    pb[bsc][:, 8:16], wexp8[:], c_eye_f[0:8, 0:8]), wait=[t_wex])
                gates["wexp"] = t_t2
                t_t3 = pe(bsc, lambda e, cs=cs, bsc=bsc: e.transpose(
                    pb[bsc][:, 16:24], dt_fm[:, cs:cs + 128], c_eye_f[0:8, 0:8]), wait=[tok_dt])
                t_t4 = pe(bsc, lambda e, bsc=bsc: e.transpose(
                    pb[bsc][:, 24:32], decb8[:], c_eye_f[0:8, 0:8]), wait=[t_decb])
                t_sm = bread(bsc, "vector", lambda e, smb=smb, bsc=bsc: e.tensor_copy(
                    scn_sm[:, smb:smb + 32], pb[bsc][:, 0:32]))
                t_sm = ev("vector", lambda e, smb=smb: e.tensor_scalar(
                    out=scn_sm[:, smb:smb + 8], in0=scn_sm[:, smb:smb + 8],
                    scalar1=30.0, scalar2=None, op0=ALU.subtract), wait=[])
                tok_ct = None
                for j in range(4):
                    t_et = pe(bet, lambda e, j=j, bet=bet: e.matmul(
                        pb[bet][:, j * 128:(j + 1) * 128], c_selc[:, j * 128:(j + 1) * 128],
                        erow[:], start=True, stop=True),
                        wait=[t_er] if j == 0 else [])
                    tok_ct = bread(bet, "vector", lambda e, j=j, cs=cs, bet=bet: e.tensor_mul(
                        ctil[:, j * TILE + cs:j * TILE + cs + 128],
                        cmat[:, j * TILE + cs:j * TILE + cs + 128],
                        pb[bet][:, j * 128:(j + 1) * 128]),
                        wait=[tok_cm[j]] + ([gates["ctil"]] if (j == 0 and c == 0 and gates["ctil"]) else []))
                gates["erow"] = t_et
                for h in range(HEADS):
                    barrier(16)
                    hp = h % 2
                    j, rr = h // 2, (h % 2) * STATE
                    bT = 0 + hp
                    bG = 4 + hp
                    bY = 6 + hp
                    bh = b * HEADS + h
                    so = (b * 4 + h // 2) * HD
                    t_gt = pe(bG, lambda e, j=j, rr=rr, cs=cs, bG=bG: e.matmul(
                        pb[bG][:, 0:128],
                        bmat[rr:rr + STATE, j * TILE + cs:j * TILE + cs + 128],
                        cmat[rr:rr + STATE, j * TILE + cs:j * TILE + cs + 128],
                        start=True, stop=True),
                        wait=[tok_bm[j], tok_cm[j]])
                    t_d0 = pe(bG, lambda e, h=h, cs=cs, bG=bG: e.matmul(
                        pb[bG][:, 128:256], c_selh[:, h * 128:(h + 1) * 128],
                        alpha[:, cs:cs + 128], start=True, stop=True),
                        wait=[t_al])
                    t_msk = bread(bG, "vector", lambda e, h=h, hp=hp, smb=smb, bG=bG:
                                  e.scalar_tensor_tensor(
                                      out=e_f32[:, hp * 128:(hp + 1) * 128],
                                      in0=pb[bG][:, 128:256],
                                      scalar=scn_sm[:, smb + h:smb + h + 1],
                                      in1=c_triu[:], op0=ALU.subtract, op1=ALU.mult),
                                  wait=[t_sm] + ([gates["ef"][hp]] if gates["ef"][hp] else []))
                    t_exp = ev("scalar", lambda e, hp=hp: e.activation(
                        out=e_f32[:, hp * 128:(hp + 1) * 128],
                        in_=e_f32[:, hp * 128:(hp + 1) * 128], func=AF.Exp, bias=cc(0)),
                        wait=[t_msk, tok_consts])
                    t_mt = bread(bG, "vector", lambda e, hp=hp, bG=bG: e.tensor_mul(
                        mt_bf[:, hp * 128:(hp + 1) * 128], pb[bG][:, 0:128],
                        e_f32[:, hp * 128:(hp + 1) * 128]),
                        wait=[t_exp] + ([gates["mt"][hp]] if gates["mt"][hp] else []))
                    gates["ef"][hp] = t_mt
                    t_xt = pe(bT, lambda e, h=h, cs=cs, bT=bT: e.transpose(
                        pbbf[bT][:, 0:128], xc_bf[:, (2 * h) * TILE + cs:(2 * h) * TILE + cs + 128],
                        c_eye_bf[:]), wait=[tok_conv_last])
                    t_xt = pe(bT, lambda e, h=h, cs=cs, bT=bT: e.transpose(
                        pbbf[bT][:, 128:256], xc_bf[:, (2 * h + 1) * TILE + cs:(2 * h + 1) * TILE + cs + 128],
                        c_eye_bf[:]))
                    t_bt = pe(bT, lambda e, j=j, rr=rr, cs=cs, bT=bT: e.transpose(
                        pbbf[bT][:, 256:256 + STATE],
                        bmat[rr:rr + STATE, j * TILE + cs:j * TILE + cs + 128],
                        c_eye_bf[rr:rr + STATE, rr:rr + STATE]), wait=[tok_bm[j]])
                    t_xtm = bread(bT, "vector", lambda e, hp=hp, smb=smb, h=h, bT=bT: e.tensor_scalar(
                        out=xtm_bf[:, hp * HD:(hp + 1) * HD], in0=pbbf[bT][:, 0:256],
                        scalar1=scn_sm[:, smb + 16 + h:smb + 17 + h], scalar2=None,
                        op0=ALU.mult),
                        wait=[t_sm] + ([gates["xtm"][hp]] if gates["xtm"][hp] else []))
                    t_bw = bread(bT, "vector", lambda e, hp=hp, smb=smb, h=h, bT=bT: e.tensor_scalar(
                        out=bw_bf[:, hp * STATE:(hp + 1) * STATE], in0=pbbf[bT][:, 256:256 + STATE],
                        scalar1=scn_sm[:, smb + 8 + h:smb + 9 + h], scalar2=None,
                        op0=ALU.mult),
                        wait=[gates["bw"][hp]] if gates["bw"][hp] else [])
                    def ymms(e, hp=hp, cs=cs, j=j, rr=rr, so=so, bY=bY):
                        if scan_mask & 1:
                            e.matmul(pb[bY][:, 0:128], xtm_bf[:, hp * HD:hp * HD + 128],
                                     mt_bf[:, hp * 128:(hp + 1) * 128],
                                     start=True, stop=not (scan_mask & 2))
                        if scan_mask & 2:
                            e.matmul(pb[bY][:, 0:128], st_bf[rr:rr + STATE, so:so + 128],
                                     ctil[rr:rr + STATE, j * TILE + cs:j * TILE + cs + 128],
                                     start=not (scan_mask & 1), stop=True)
                        if scan_mask & 1:
                            e.matmul(pb[bY][:, 128:256], xtm_bf[:, hp * HD + 128:hp * HD + 256],
                                     mt_bf[:, hp * 128:(hp + 1) * 128],
                                     start=True, stop=not (scan_mask & 2))
                        if scan_mask & 2:
                            e.matmul(pb[bY][:, 128:256], st_bf[rr:rr + STATE, so + 128:so + 256],
                                     ctil[rr:rr + STATE, j * TILE + cs:j * TILE + cs + 128],
                                     start=not (scan_mask & 1), stop=True)
                        if rr == 0:
                            return e.matmul(pb[bY][0:STATE, 256:512],
                                            bw_bf[:, hp * STATE:(hp + 1) * STATE],
                                            xtm_bf[:, hp * HD:(hp + 1) * HD],
                                            start=True, stop=True)
                        return e.matmul(pb[bY][rr:rr + STATE, 256:512],
                                        bw_bf[:, hp * STATE:(hp + 1) * STATE],
                                        xtm_bf[:, hp * HD:(hp + 1) * HD],
                                        start=True, stop=True, tile_position=(0, rr))
                    t_y = pe(bY, ymms, wait=[t_xtm, t_bw, t_mt, tok_ct, gates["state"][bh]])
                    gates["xtm"][hp] = t_y
                    gates["bw"][hp] = t_y
                    gates["mt"][hp] = t_y
                    t_su = bread(bY, "vector", lambda e, so=so, rr=rr, smb=smb, h=h, bY=bY:
                                 e.scalar_tensor_tensor(
                                     out=st_f32[rr:rr + STATE, so:so + HD],
                                     in0=st_f32[rr:rr + STATE, so:so + HD],
                                     scalar=scn_sm[rr:rr + STATE, smb + 24 + h:smb + 25 + h],
                                     in1=pb[bY][rr:rr + STATE, 256:512],
                                     op0=ALU.mult, op1=ALU.add))
                    gates["state"][bh] = ev("vector", lambda e, so=so, rr=rr: e.tensor_copy(
                        st_bf[rr:rr + STATE, so:so + HD], st_f32[rr:rr + STATE, so:so + HD]),
                        wait=[])
                    for pi in range(2):
                        t_ye = bread(bY, "vector", lambda e, h=h, pi=pi, cs=cs, bY=bY:
                                     e.scalar_tensor_tensor(
                                         out=y_sb[:, (2 * h + pi) * 128:(2 * h + pi) * 128 + 128],
                                         in0=xc_bf[:, (2 * h + pi) * TILE + cs:
                                                   (2 * h + pi) * TILE + cs + 128],
                                         scalar=dT[:, h:h + 1] if (scan_mask & 4) else cc(1),
                                         in1=pb[bY][:, pi * 128:(pi + 1) * 128],
                                         op0=ALU.mult if (scan_mask & 4) else ALU.bypass,
                                         op1=ALU.add),
                                     wait=[tok_dT] +
                                          (_flat([gates["y"], gates["ysq"]])
                                           if (h == 0 and pi == 0) else []))
                yv = y_sb[:]
                gates["y"] = dma("sync",
                                 y_sp[b, :, :, t0 + cs:t0 + cs + 128].rearrange("c p t -> p c t"),
                                 yv.rearrange("p (c t) -> p c t", c=KI),
                                 wait=[t_ye])
                if "y" in DT:
                    gates["y2"] = tap_toks[-0:] and None
                    _ytap = dma("sync",
                                taps["y"][b, :, :, t0 + cs:t0 + cs + 128].rearrange("c p t -> p c t"),
                                yv.rearrange("p (c t) -> p c t", c=KI),
                                wait=[t_ye], sem=qT)
                    tap_toks.append(_ytap)
                    gates["y"] = (gates["y"], _ytap)
                t_st1 = ev("vector", lambda e, yv=yv: e.reduce_sum(
                    out=rtmp[:, 48:64], in_=yv.rearrange("p (c t) -> p c t", c=KI),
                    axis=AX.X), wait=[])
                t_st2 = ev("vector", lambda e, b=b: e.tensor_add(
                    stats_sum[:, b * KI:(b + 1) * KI], stats_sum[:, b * KI:(b + 1) * KI],
                    rtmp[:, 48:64]), wait=[tok_consts])
                t_sq1 = ev("scalar", lambda e, yv=yv: e.activation(
                    out=sq_bf[:], in_=yv, func=AF.Square),
                    wait=[t_ye] + ([gates["sq"]] if gates["sq"] else []))
                gates["ysq"] = t_sq1
                t_sq2 = ev("vector", lambda e: e.reduce_sum(
                    out=rtmp[:, 16:32], in_=sq_bf.ap().rearrange("p (c t) -> p c t", c=KI),
                    axis=AX.X), wait=[t_sq1])
                gates["sq"] = t_sq2
                t_sq3 = ev("vector", lambda e, b=b: e.tensor_add(
                    stats_sq[:, b * KI:(b + 1) * KI], stats_sq[:, b * KI:(b + 1) * KI],
                    rtmp[:, 16:32]), wait=[tok_consts])
            gates["xc_t"] = t_xt
            gates["xc_v"] = t_ye
            gates["bc"] = t_y
            gates["ctil"] = t_y
            gates["aline"] = t_y
            gates["am1"] = t_t2

    t_statv = p.last["vector"]
    for b in range(nb):
        t_tr1 = pe(2, lambda e, b=b: e.transpose(
            pb[2][0:KI, 0:128], stats_sum[:, b * KI:(b + 1) * KI], c_eye_f[:]),
            wait=[t_statv])
        t_tr2 = pe(2, lambda e, b=b: e.transpose(
            pb[2][0:KI, 128:256], stats_sq[:, b * KI:(b + 1) * KI], c_eye_f[:]))
        t_r1 = bread(2, "vector", lambda e: e.reduce_sum(
            out=rtmp[0:KI, 48:49], in_=pb[2][0:KI, 0:128], axis=AX.X))
        t_r2 = bread(2, "vector", lambda e: e.reduce_sum(
            out=rtmp[0:KI, 49:50], in_=pb[2][0:KI, 128:256], axis=AX.X))
        t_pm = pe(3, lambda e: e.matmul(pb[3][0:HEADS, 0:2], c_pair16[:, 0:HEADS],
                                        rtmp[0:KI, 48:50], start=True, stop=True),
                  wait=[t_r1, t_r2])
        t_ev = bread(3, "vector", lambda e: e.tensor_copy(rtmp8[:, 130:132], pb[3][0:HEADS, 0:2]))
        ev("vector", lambda e: e.tensor_scalar(out=rtmp8[:, 132:133], in0=rtmp8[:, 130:131],
                                               scalar1=1.0 / NORM_GN, scalar2=None, op0=ALU.mult))
        ev("vector", lambda e: e.tensor_scalar(out=rtmp8[:, 133:134], in0=rtmp8[:, 131:132],
                                               scalar1=1.0 / NORM_GN, scalar2=None, op0=ALU.mult))
        ev("vector", lambda e: e.tensor_mul(rtmp8[:, 134:135], rtmp8[:, 132:133], rtmp8[:, 132:133]))
        t_v8 = ev("vector", lambda e: e.tensor_sub(rtmp8[:, 135:136], rtmp8[:, 133:134],
                                                   rtmp8[:, 134:135]))
        t_l8 = ev("scalar", lambda e: e.activation(out=rtmp8[:, 136:137], in_=rtmp8[:, 135:136],
                                                   func=AF.Ln, bias=cc(8, HEADS)), wait=[t_v8])
        t_r8 = ev("scalar", lambda e: e.activation(out=rtmp8[:, 137:138], in_=rtmp8[:, 136:137],
                                                   func=AF.Exp, scale=cc(3, HEADS)))
        t_m8 = ev("vector", lambda e: e.tensor_mul(rtmp8[:, 138:139], rtmp8[:, 132:133],
                                                   rtmp8[:, 137:138]), wait=[t_r8])
        t_x1 = pe(3, lambda e: e.matmul(pb[3][0:1, 16:32], rtmp8[0:HEADS, 137:138],
                                        c_dblrow[:], start=True, stop=True),
                  wait=[t_m8])
        t_x2m = pe(3, lambda e: e.matmul(pb[3][0:1, 32:48], rtmp8[0:HEADS, 138:139],
                                         c_dblrow[:], start=True, stop=True))
        t_rr = bread(3, "vector", lambda e: e.tensor_copy(rtmp[0:1, 16:32], pb[3][0:1, 16:32]))
        t_rr = bread(3, "vector", lambda e: e.tensor_copy(rtmp[0:1, 32:48], pb[3][0:1, 32:48]))
        t_bc1 = pe(4, lambda e: e.matmul(pb[4][:, 0:16], c_ones_row_f[:], rtmp[0:1, 16:32],
                                         start=True, stop=True), wait=[t_rr])
        t_bc2 = pe(4, lambda e: e.matmul(pb[4][:, 16:32], c_ones_row_f[:], rtmp[0:1, 32:48],
                                         start=True, stop=True))
        bread(4, "vector", lambda e, b=b: e.tensor_mul(
            gn_scale[:, b * KI:(b + 1) * KI], gnw_col[:], pb[4][:, 0:16]))
        bread(4, "vector", lambda e: e.tensor_mul(rtmp[:, 48:64], gnw_col[:], pb[4][:, 16:32]))
        ev("vector", lambda e, b=b: e.tensor_sub(
            gn_bias[:, b * KI:(b + 1) * KI], gnb_col[:], rtmp[:, 48:64]))
    t_b2T = tok_consts
    if stop_after == "p15":
        return _finish()

    qWo, qY = sems["qWo"], [sems["qY0"], sems["qY1"]]
    qZ = [sems["qZ0"], sems["qZ1"]]
    qL, qF, qAC = sems["qL"], sems["qF"], sems["qAC"]
    qR1, qR2, qO = sems["qR1"], sems["qR2"], sems["qO"]

    j_p1 = [gates["aline"], p.last["vector"], p.last["scalar"],
            (qS, qS.n), (qA, qA.n), (qX[0], qX[0].n), (qX[1], qX[1].n),
            (qT, qT.n), (qG, qG.n)]
    tok_wo = None
    for k in range(KI):
        tok_wo = dma("gpsimd", woutT[:, k * DIM:(k + 1) * DIM],
                     Wout[k * 128:(k + 1) * 128, :], sem=qWo,
                     wait=j_p1 if k == 0 else [])
    if stop_after == "p2_wo":
        return _finish()

    def p2stage(stage):
        if not (isinstance(stop_after, str) and stop_after.startswith("p2_")):
            return True
        order = ["gn", "x", "mm", "ln", "l2", "sp"]
        return order.index(stage) <= order.index(stop_after[3:])

    gates2 = {"yld": [None, None], "szld": [None, None], "x2tm": [None, None],
              "xfm": None, "u": None, "attn": None, "x2spill": None,
              "ln2spill": None, "x2f": None, "ln2b": None, "sqr": None}
    tok_x2sp = []
    tok_ln2sp = []
    for b in range(nb):
        for ti in range(nt):
            t0 = ti * TILE
            t_y4 = [None, None]
            t_z4 = [None, None]
            for g in range(2):
                t_y4[g] = dma("sync", y_ld[:, g * 4 * TILE:(g + 1) * 4 * TILE].rearrange(
                    "p (c t) -> p c t", c=4),
                    y_sp[b, g * 4:(g + 1) * 4, :, t0:t0 + TILE].rearrange("c p t -> p c t"),
                    sem=qY[g], wait=(j_p1 if (b == 0 and ti == 0) else []) +
                                    ([gates2["yld"][g]] if gates2["yld"][g] else []))
                t_z4[g] = dma("sync", sz_ld[:, g * 4 * TILE:(g + 1) * 4 * TILE].rearrange(
                    "p (c t) -> p c t", c=4),
                    sz_sp[b, g * 4:(g + 1) * 4, :, t0:t0 + TILE].rearrange("c p t -> p c t"),
                    sem=qZ[g], wait=[gates2["szld"][g]] if gates2["szld"][g] else [])
            t_y4b = [None, None]
            t_z4b = [None, None]
            t_uc = [None] * KI
            for c in range(KI):
                g = (c // 4) % 2
                if c == 8:
                    for g2 in range(2):
                        t_y4b[g2] = dma("sync", y_ld[:, g2 * 4 * TILE:(g2 + 1) * 4 * TILE].rearrange(
                            "p (c t) -> p c t", c=4),
                            y_sp[b, 8 + g2 * 4:8 + (g2 + 1) * 4, :, t0:t0 + TILE].rearrange("c p t -> p c t"),
                            sem=qY[g2], wait=[t_uc[3 + g2 * 4]])
                        t_z4b[g2] = dma("sync", sz_ld[:, g2 * 4 * TILE:(g2 + 1) * 4 * TILE].rearrange(
                            "p (c t) -> p c t", c=4),
                            sz_sp[b, 8 + g2 * 4:8 + (g2 + 1) * 4, :, t0:t0 + TILE].rearrange("c p t -> p c t"),
                            sem=qZ[g2], wait=[t_uc[3 + g2 * 4]])
                yw = (t_y4b[g] if c >= 8 else t_y4[g])
                zw = (t_z4b[g] if c >= 8 else t_z4[g])
                sl = (c % 8) * TILE
                t_gn = ev("vector", lambda e, c=c, b=b, sl=sl: e.tensor_scalar(
                    out=gnt_bf[:], in0=y_ld[:, sl:sl + TILE],
                    scalar1=gn_scale[:, b * KI + c:b * KI + c + 1],
                    scalar2=gn_bias[:, b * KI + c:b * KI + c + 1],
                    op0=ALU.mult, op1=ALU.add), wait=[yw, t_b2T])
                t_uc[c] = ev("vector", lambda e, c=c, sl=sl: e.tensor_mul(
                    u_bf[:, c * TILE:(c + 1) * TILE], gnt_bf[:], sz_ld[:, sl:sl + TILE]),
                    wait=[zw])
            if not p2stage("x"):
                continue
            t_xf = None
            for s in range(NSUB):
                par = s % 2
                xv = x_tm2[:, par * DIM:(par + 1) * DIM]
                t_xl = dma("sync", xv, x_in[b, t0 + s * 128:t0 + (s + 1) * 128, :],
                           sem=qX[par], wait=[gates2["x2tm"][par]] if gates2["x2tm"][par] else [])
                t_tt = None
                for k in range(KD):
                    bk = 6 + (k % 2)
                    slq = k // 2
                    t_tt = pe(bk, lambda e, xv=xv, k=k, slq=slq, bk=bk: e.transpose(
                        pb[bk][:, slq * 128:(slq + 1) * 128],
                        xv[:, k * 128:(k + 1) * 128], c_eye_f[:]),
                        wait=[t_xl] if k == 0 else [])
                gates2["x2tm"][par] = t_tt
                xfv = x_fm.ap().rearrange("p (k4 half t) -> p k4 half t", k4=4, half=2)
                for half in range(2):
                    t_xf = bread(6 + half, "vector",
                                 lambda e, s=s, half=half, xfv=xfv: e.tensor_copy(
                                     xfv[:, :, half, s * 128:(s + 1) * 128],
                                     pb[6 + half][:, 0:512].rearrange("p (k t) -> p k t", k=4)),
                                 wait=[gates2["xfm"]] if (s == 0 and half == 0 and gates2["xfm"]) else [])
            if not p2stage("mm"):
                continue
            t_att = [None] * KD
            for o in range(KD):
                bk = o % 2
                t_mm = pe(bk, lambda e, o=o, bk=bk: [
                    e.matmul(pb[bk][:, 0:512], woutT[:, k * DIM + o * 128:k * DIM + (o + 1) * 128],
                             u_bf[:, k * TILE:(k + 1) * TILE],
                             start=(k == 0), stop=(k == KI - 1)) for k in range(KI)][-1],
                    wait=[t_uc[KI - 1], tok_wo] if o == 0 else [])
                t_af = bread(bk, "vector", lambda e, o=o, bk=bk: e.tensor_copy(
                    attn_f[:, o * TILE:(o + 1) * TILE], pb[bk][:, 0:512]),
                    wait=[gates2["attn"]] if (o == 0 and gates2["attn"]) else [])
                t_ab2 = ev("scalar", lambda e, o=o: e.activation(
                    out=attn_bf[:, o * TILE:(o + 1) * TILE],
                    in_=attn_f[:, o * TILE:(o + 1) * TILE],
                    func=AF.Identity), wait=[t_af])
                t_sq2 = ev("scalar", lambda e, o=o: e.activation(
                    out=sq2_bf[:, o * TILE:(o + 1) * TILE],
                    in_=attn_f[:, o * TILE:(o + 1) * TILE],
                    func=AF.Square))
                t_att[o] = t_sq2
            t_s1 = pe(2, lambda e: [
                e.matmul(pb[2][0:1, 0:512], c_ones_col_bf[:], attn_bf[:, k * TILE:(k + 1) * TILE],
                         start=(k == 0), stop=(k == KD - 1)) for k in range(KD)][-1],
                wait=[t_att[KD - 1]])
            t_s2 = pe(3, lambda e: [
                e.matmul(pb[3][0:1, 0:512], c_ones_col_bf[:], sq2_bf[:, k * TILE:(k + 1) * TILE],
                         start=(k == 0), stop=(k == KD - 1)) for k in range(KD)][-1])

            def ln_rows(bmu, bsq):
                bread(bmu, "vector", lambda e: e.tensor_scalar(
                    out=rows_sb[:, 0:TILE], in0=pb[bmu][0:1, 0:512],
                    scalar1=1.0 / DIM, scalar2=None, op0=ALU.mult))
                ev("vector", lambda e: e.tensor_mul(rows_sb[:, TILE:2 * TILE],
                                                    rows_sb[:, 0:TILE], rows_sb[:, 0:TILE]))
                bread(bsq, "vector", lambda e: e.scalar_tensor_tensor(
                    out=rows_sb[:, TILE:2 * TILE], in0=pb[bsq][0:1, 0:512],
                    scalar=1.0 / DIM, in1=rows_sb[:, TILE:2 * TILE],
                    op0=ALU.mult, op1=ALU.subtract))
                t_ln = ev("scalar", lambda e: e.activation(
                    out=rows_sb[:, TILE:2 * TILE], in_=rows_sb[:, TILE:2 * TILE],
                    func=AF.Ln, bias=cc(8, 1)), wait=[p.last["vector"]])
                t_A = ev("scalar", lambda e: e.activation(
                    out=rows_sb[:, 2 * TILE:3 * TILE], in_=rows_sb[:, TILE:2 * TILE],
                    func=AF.Exp, scale=cc(3, 1)))
                t_B2 = ev("vector", lambda e: e.scalar_tensor_tensor(
                    out=rows_sb[:, 3 * TILE:4 * TILE], in0=rows_sb[:, 0:TILE],
                    scalar=-1.0, in1=rows_sb[:, 2 * TILE:3 * TILE],
                    op0=ALU.mult, op1=ALU.mult), wait=[t_A])
                t_bA = pe(4, lambda e: e.matmul(pb[4][:, 0:512], c_ones_row_f[:],
                                                rows_sb[:, 2 * TILE:3 * TILE],
                                                start=True, stop=True), wait=[t_B2])
                t_bB = pe(5, lambda e: e.matmul(pb[5][:, 0:512], c_ones_row_f[:],
                                                rows_sb[:, 3 * TILE:4 * TILE],
                                                start=True, stop=True), wait=[t_B2])
                return t_bA, t_bB
            if not p2stage("ln"):
                continue
            ln_rows(2, 3)
            xfv2 = x_fm.ap().rearrange("p (k4 half t) -> p k4 half t", k4=4, half=2)
            t_x2 = None
            for o in range(KD):
                k4, half = o // 2, o % 2
                t_t1 = bread(4, "vector", lambda e, o=o: e.tensor_mul(
                    x2_f[:, o * TILE:(o + 1) * TILE], attn_f[:, o * TILE:(o + 1) * TILE],
                    pb[4][:, 0:512]), wait=[t_xf, gates2["x2f"]] if o == 0 else [])
                t_t2 = bread(5, "vector", lambda e, o=o: e.tensor_add(
                    x2_f[:, o * TILE:(o + 1) * TILE], x2_f[:, o * TILE:(o + 1) * TILE],
                    pb[5][:, 0:512]))
                t_x2 = ev("vector", lambda e, o=o, k4=k4, half=half, xfv2=xfv2: e.tensor_add(
                    x2_f[:, o * TILE:(o + 1) * TILE], x2_f[:, o * TILE:(o + 1) * TILE],
                    xfv2[:, k4, half, :]))
            gates2["xfm"] = t_x2
            if not p2stage("l2"):
                continue
            t_att2 = [None] * KD
            for o in range(KD):
                t_c1 = ev("scalar", lambda e, o=o: e.activation(
                    out=attn_bf[:, o * TILE:(o + 1) * TILE], in_=x2_f[:, o * TILE:(o + 1) * TILE],
                    func=AF.Identity), wait=[t_x2] if o == 0 else [])
                t_att2[o] = ev("scalar", lambda e, o=o: e.activation(
                    out=sq2_bf[:, o * TILE:(o + 1) * TILE], in_=x2_f[:, o * TILE:(o + 1) * TILE],
                    func=AF.Square))
            t_s1b = pe(2, lambda e: [
                e.matmul(pb[2][0:1, 0:512], c_ones_col_bf[:], attn_bf[:, k * TILE:(k + 1) * TILE],
                         start=(k == 0), stop=(k == KD - 1)) for k in range(KD)][-1],
                wait=[t_att2[KD - 1]])
            t_s2b = pe(3, lambda e: [
                e.matmul(pb[3][0:1, 0:512], c_ones_col_bf[:], sq2_bf[:, k * TILE:(k + 1) * TILE],
                         start=(k == 0), stop=(k == KD - 1)) for k in range(KD)][-1])
            ln_rows(2, 3)
            t_ln2 = None
            for o in range(KD):
                t_t1 = bread(4, "vector", lambda e, o=o: e.tensor_mul(
                    x2_f[:, o * TILE:(o + 1) * TILE] if False else attn_f[:, o * TILE:(o + 1) * TILE],
                    x2_f[:, o * TILE:(o + 1) * TILE], pb[4][:, 0:512]),
                    wait=[gates2["ln2b"]] if (o == 0 and gates2["ln2b"]) else [])
                t_ln2 = bread(5, "vector", lambda e, o=o: e.tensor_add(
                    ln2_bf[:, o * TILE:(o + 1) * TILE], attn_f[:, o * TILE:(o + 1) * TILE],
                    pb[5][:, 0:512]))
                ev("vector", lambda e, o=o: e.reduce_sum(
                    out=rtmp[:, 48 + o:49 + o], in_=ln2_bf[:, o * TILE:(o + 1) * TILE],
                    axis=AX.X))
                ev("vector", lambda e, o=o, b=b: e.tensor_add(
                    racc[:, b * KD + o:b * KD + o + 1], racc[:, b * KD + o:b * KD + o + 1],
                    rtmp[:, 48 + o:49 + o]))
            if not p2stage("sp"):
                continue
            gates2["x2spill"] = dma("scalar", x2_sp[b, :, :, t0:t0 + TILE].rearrange("c p t -> p c t"),
                                    x2_f.ap().rearrange("p (c t) -> p c t", c=KD),
                                    sem=qA, wait=[t_ln2])
            tok_x2sp.append(gates2["x2spill"])
            gates2["ln2spill"] = dma("sync", ln2_sp[b, :, :, t0:t0 + TILE].rearrange("c p t -> p c t"),
                                     ln2_bf.ap().rearrange("p (c t) -> p c t", c=KD),
                                     sem=qS, wait=[t_ln2])
            tok_ln2sp.append(gates2["ln2spill"])
            gates2["x2f"] = gates2["x2spill"]
            gates2["ln2b"] = gates2["ln2spill"]
            gates2["attn"] = t_ln2
            gates2["yld"] = [t_uc[11], t_uc[15]]
            gates2["szld"] = [t_uc[11], t_uc[15]]
            gates2["sqr"] = t_att2[KD - 1]

    if stop_after == "p2" or (isinstance(stop_after, str) and stop_after.startswith("p2_")):
        return _finish()

    for b in range(nb):
        t_rep = ev("vector", lambda e, b=b: e.tensor_scalar(
            out=rtmp[:, 16:24], in0=racc[:, b * KD:(b + 1) * KD],
            scalar1=1.0 / S, scalar2=None, op0=ALU.mult))
        t_lg = pe(2, lambda e: [
            e.matmul(pb[2][0:1, 0:8], rtmp[:, 16 + c:17 + c], gate_sb[:, c * 8:(c + 1) * 8],
                     start=(c == 0), stop=(c == KD - 1)) for c in range(KD)][-1],
            wait=[t_rep])
        t_lr = bread(2, "vector", lambda e: e.tensor_copy(r8_sb[:, 0:8], pb[2][0:1, 0:8]))
        t_mx = ev("vector", lambda e: e.reduce_max(out=r8_sb[:, 8:9], in_=r8_sb[:, 0:8], axis=AX.X))
        t_nm = ev("vector", lambda e: e.tensor_scalar(out=r8_sb[:, 9:10], in0=r8_sb[:, 8:9],
                                                      scalar1=-1.0, scalar2=None, op0=ALU.mult))
        t_e8 = ev("scalar", lambda e: e.activation(out=r8_sb[:, 16:24], in_=r8_sb[:, 0:8],
                                                   func=AF.Exp, bias=r8_sb[:, 9:10]),
                  wait=[t_nm])
        t_m1 = ev("vector", lambda e: e.reduce_max(out=r8_sb[:, 24:25], in_=r8_sb[:, 16:24],
                                                   axis=AX.X), wait=[t_e8])
        t_g1 = ev("vector", lambda e: e.tensor_scalar(out=r8_sb[:, 32:40], in0=r8_sb[:, 16:24],
                                                      scalar1=r8_sb[:, 24:25], scalar2=None,
                                                      op0=ALU.is_ge))
        t_mk = ev("vector", lambda e: e.scalar_tensor_tensor(
            out=r8_sb[:, 40:48], in0=r8_sb[:, 32:40], scalar=-1e30, in1=r8_sb[:, 16:24],
            op0=ALU.mult, op1=ALU.add))
        t_m2 = ev("vector", lambda e: e.reduce_max(out=r8_sb[:, 25:26], in_=r8_sb[:, 40:48],
                                                   axis=AX.X))
        t_g2 = ev("vector", lambda e: e.tensor_scalar(out=r8_sb[:, 48:56], in0=r8_sb[:, 16:24],
                                                      scalar1=r8_sb[:, 25:26], scalar2=None,
                                                      op0=ALU.is_ge))
        t_w8 = ev("vector", lambda e: e.tensor_mul(r8_sb[:, 56:64], r8_sb[:, 16:24],
                                                   r8_sb[:, 48:56]))
        t_sm8 = ev("vector", lambda e: e.reduce_sum(out=r8_sb[:, 26:27], in_=r8_sb[:, 56:64],
                                                    axis=AX.X))
        t_li = ev("scalar", lambda e: e.activation(out=r8_sb[:, 27:28], in_=r8_sb[:, 26:27],
                                                   func=AF.Ln), wait=[t_sm8])
        t_iv = ev("scalar", lambda e: e.activation(out=r8_sb[:, 28:29], in_=r8_sb[:, 27:28],
                                                   func=AF.Exp, scale=cc(2, 1)))
        t_cb = ev("vector", lambda e: e.tensor_scalar(out=r8_sb[:, 56:64], in0=r8_sb[:, 56:64],
                                                      scalar1=r8_sb[:, 28:29], scalar2=None,
                                                      op0=ALU.mult), wait=[t_iv])
        t_cc2 = pe(4, lambda e: e.matmul(pb[4][:, 0:8], c_ones_row_f[:], r8_sb[:, 56:64],
                                         start=True, stop=True), wait=[t_cb])
        t_cbc = bread(4, "vector", lambda e, b=b: e.tensor_copy(
            comb_cols[:, b * NE:(b + 1) * NE], pb[4][:, 0:8]))
        for c in range(KD):
            t_ac = ev("vector", lambda e, c=c, b=b: e.tensor_scalar(
                out=b2c_cols[:, b * KD + c:b * KD + c + 1],
                in0=b2T[:, c * 8:c * 8 + 1],
                scalar1=comb_cols[:, b * NE:b * NE + 1], scalar2=None, op0=ALU.mult))
            for e2 in range(1, NE):
                t_ac = ev("vector", lambda e, c=c, b=b, e2=e2: e.scalar_tensor_tensor(
                    out=b2c_cols[:, b * KD + c:b * KD + c + 1],
                    in0=b2T[:, c * 8 + e2:c * 8 + e2 + 1],
                    scalar=comb_cols[:, b * NE + e2:b * NE + e2 + 1],
                    in1=b2c_cols[:, b * KD + c:b * KD + c + 1],
                    op0=ALU.mult, op1=ALU.add))

    if stop_after == "i2":
        return _finish()

    j_p2 = [p.last["tensor"], p.last["vector"], p.last["scalar"],
            (qA, qA.n), (qS, qS.n), (qY[0], qY[0].n), (qY[1], qY[1].n),
            (qZ[0], qZ[0].n), (qZ[1], qZ[1].n), (qX[0], qX[0].n), (qX[1], qX[1].n),
            (qWo, qWo.n)]
    g3 = {"w1": None, "w2": None, "ln2t": None, "ffn": None,
          "w1_rd": None, "w2_rd": None, "hm": None}
    MG = KF // 8
    rel_w1 = [None] * MG
    tok_w1g = [None] * MG
    for e_i in range(ne3):
        tok_w2 = None
        for mg in range(MG):
            tok_w1g[mg] = dma(
                "scalar",
                w1_bf[:, mg * 8192:(mg + 1) * 8192].rearrange("p (k f) -> p k f", k=KD),
                w1b[e_i, :, mg * 1024:(mg + 1) * 1024].rearrange("(k p) f -> p k f", p=128),
                sem=qF,
                wait=(j_p2 + [tok_wb] if (e_i == 0 and mg == 0) else []) +
                     ([rel_w1[mg]] if rel_w1[mg] else []))
        for g in range(KF // 4):
            tok_w2 = dma("scalar", w2_bf[:, g * 4 * DIM:(g + 1) * 4 * DIM].rearrange(
                "p (k f) -> p k f", k=4),
                w2b[e_i, g * 512:(g + 1) * 512, :].rearrange("(k p) f -> p k f", p=128),
                sem=qF, wait=[g3["w2_rd"]] if (g == 0 and g3["w2_rd"]) else [])
        for gt in range(nb * nt):
            b, ti = gt // NT, gt % NT
            t0 = ti * TILE
            tok_l = dma("sync", ln2t[:].rearrange("p (c t) -> p c t", c=KD),
                        ln2_sp[b, :, :, t0:t0 + TILE].rearrange("c p t -> p c t"),
                        sem=qL, wait=[g3["ln2t"]] if g3["ln2t"] else
                        [tok_ln2sp[-1], (qS, qS.n)])
            t_hm = [None] * KF
            t_mmA = None
            for m in range(KF):
                bk = m % 3
                mg, mw = m // 8, m % 8
                t_mmA = pe(bk, lambda e, mg=mg, mw=mw, bk=bk: [
                    e.matmul(pb[bk][:, 0:512],
                             w1_bf[:, mg * 8192 + k * 1024 + mw * 128:
                                   mg * 8192 + k * 1024 + (mw + 1) * 128],
                             ln2t[:, k * TILE:(k + 1) * TILE],
                             start=(k == 0), stop=(k == KD - 1)) for k in range(KD)][-1],
                    wait=([tok_l] if m == 0 else []) +
                         ([tok_w1g[mg]] if mw == 0 else []))
                t_hm[m] = bread(bk, "scalar", lambda e, m=m, bk=bk, e_i=e_i: e.activation(
                    out=hmid_bf[:, m * TILE:(m + 1) * TILE], in_=pb[bk][:, 0:512],
                    func=AF.Gelu, bias=b1_cols[:, e_i * KF + m:e_i * KF + m + 1]),
                    wait=[g3["hm"]] if (m == 0 and g3["hm"]) else [])
                if gt == nb * nt - 1 and mw == 7:
                    rel_w1[mg] = t_mmA
            g3["ln2t"] = t_mmA
            t_ffn = None
            t_mmB = None
            for o in range(KD):
                bk = 4 + o % 2
                t_mmB = pe(bk, lambda e, o=o, bk=bk: [
                    e.matmul(pb[bk][:, 0:512],
                             w2_bf[:, kf * DIM + o * 128:kf * DIM + (o + 1) * 128],
                             hmid_bf[:, kf * TILE:(kf + 1) * TILE],
                             start=(kf == 0), stop=(kf == KF - 1)) for kf in range(KF)][-1],
                    wait=[t_hm[KF - 1], tok_w2] if o == 0 else [])
                t_ffn = bread(bk, "vector", lambda e, o=o, bk=bk, b=b, e_i=e_i: e.tensor_scalar(
                    out=ffn_sb[:, o * TILE:(o + 1) * TILE], in0=pb[bk][:, 0:512],
                    scalar1=comb_cols[:, b * NE + e_i:b * NE + e_i + 1],
                    scalar2=None, op0=ALU.mult),
                    wait=[g3["ffn"]] if (o == 0 and g3["ffn"]) else [])
            g3["hm"] = t_mmB
            g3["w2_rd"] = t_mmB if gt == nb * nt - 1 else g3["w2_rd"]
            g3["ffn"] = dma("gpsimd", acc_sp[b, :, :, t0:t0 + TILE].rearrange("c p t -> p c t"),
                            ffn_sb.ap().rearrange("p (c t) -> p c t", c=KD),
                            sem=qAC, accum=(e_i > 0), wait=[t_ffn])

    if stop_after == "p3a":
        return _finish()

    j_p3 = [p.last["tensor"], p.last["vector"], p.last["scalar"],
            (qAC, qAC.n), (qL, qL.n), (qF, qF.n)]
    g4 = {"acc": None, "x2": None, "accrd": None, "x2rd": None, "out": [None, None]}
    tok_out = []
    for gt in range(nb * nt):
        b, ti = gt // NT, gt % NT
        t0 = ti * TILE
        t_la = dma("sync", acc_ld[:].rearrange("p (c t) -> p c t", c=KD),
                   acc_sp[b, :, :, t0:t0 + TILE].rearrange("c p t -> p c t"),
                   sem=qR1, wait=(j_p3 if gt == 0 else []) +
                                 ([g4["accrd"]] if g4["accrd"] else []))
        t_lx = dma("scalar", x2_ld[:].rearrange("p (c t) -> p c t", c=KD),
                   x2_sp[b, :, :, t0:t0 + TILE].rearrange("c p t -> p c t"),
                   sem=qR2, wait=(j_p3 if gt == 0 else []) +
                                 ([g4["x2rd"]] if g4["x2rd"] else []))
        t_b2a = None
        for o in range(KD):
            t_b2a = ev("vector", lambda e, o=o, b=b: e.tensor_scalar(
                out=acc_ld[:, o * TILE:(o + 1) * TILE], in0=acc_ld[:, o * TILE:(o + 1) * TILE],
                scalar1=b2c_cols[:, b * KD + o:b * KD + o + 1], scalar2=None, op0=ALU.add),
                wait=[t_la] if o == 0 else [])
        for s in range(NSUB):
            par = s % 2
            t_ta = None
            t_txx = None
            for o in range(KD):
                bk = 6 + (o // 4)
                slq = o % 4
                t_ta = pe(bk, lambda e, o=o, s=s, slq=slq, bk=bk: e.transpose(
                    pb[bk][:, slq * 128:(slq + 1) * 128],
                    acc_ld[:, o * TILE + s * 128:o * TILE + (s + 1) * 128], c_eye_f[:]),
                    wait=[t_b2a] if o == 0 else [])
            t_ea = [None, None]
            for half in range(2):
                t_ea[half] = bread(6 + half, "vector", lambda e, half=half: e.tensor_copy(
                    acc_tm[:, half * 512:(half + 1) * 512], pb[6 + half][:, 0:512]))
            for o in range(KD):
                bk = 2 + (o // 4)
                slq = o % 4
                t_txx = pe(bk, lambda e, o=o, s=s, slq=slq, bk=bk: e.transpose(
                    pb[bk][:, slq * 128:(slq + 1) * 128],
                    x2_ld[:, o * TILE + s * 128:o * TILE + (s + 1) * 128], c_eye_f[:]),
                    wait=[t_lx] if o == 0 else [])
            t_ex = [None, None]
            for half in range(2):
                t_ex[half] = bread(2 + half, "vector", lambda e, half=half: e.tensor_copy(
                    x2_tm[:, half * 512:(half + 1) * 512], pb[2 + half][:, 0:512]))
            t_rd = ev("vector", lambda e: e.reduce_sum(out=rtmp[:, 0:1], in_=acc_tm[:],
                                                       axis=AX.X))
            t_sqf = ev("scalar", lambda e: e.activation(
                out=trash3b[:], in_=acc_tm[:],
                func=AF.Square, accum_out=rtmp[:, 8:9]),
                wait=[t_ea[1]])
            ev("vector", lambda e: e.tensor_scalar(out=rtmp[:, 1:2], in0=rtmp[:, 0:1],
                                                   scalar1=1.0 / DIM, scalar2=None, op0=ALU.mult))
            ev("vector", lambda e: e.tensor_mul(rtmp[:, 2:3], rtmp[:, 1:2], rtmp[:, 1:2]))
            t_vr = ev("vector", lambda e: e.scalar_tensor_tensor(
                out=rtmp[:, 3:4], in0=rtmp[:, 8:9], scalar=1.0 / DIM, in1=rtmp[:, 2:3],
                op0=ALU.mult, op1=ALU.subtract), wait=[t_sqf])
            t_lnf = ev("scalar", lambda e: e.activation(out=rtmp[:, 9:10], in_=rtmp[:, 3:4],
                                                        func=AF.Ln, bias=cc(8)), wait=[t_vr])
            t_rf = ev("scalar", lambda e: e.activation(out=rtmp[:, 10:11], in_=rtmp[:, 9:10],
                                                       func=AF.Exp, scale=cc(3)))
            t_bf2 = ev("vector", lambda e: e.scalar_tensor_tensor(
                out=rtmp[:, 4:5], in0=rtmp[:, 1:2], scalar=-1.0, in1=rtmp[:, 10:11],
                op0=ALU.mult, op1=ALU.mult), wait=[t_rf])
            ov = out_tm[:, par * DIM:(par + 1) * DIM]
            t_ap2 = ev("vector", lambda e, ov=ov: e.tensor_scalar(
                out=ov, in0=acc_tm[:], scalar1=rtmp[:, 10:11], scalar2=rtmp[:, 4:5],
                op0=ALU.mult, op1=ALU.add),
                wait=[t_bf2] + ([g4["out"][par]] if g4["out"][par] else []))
            t_fin = ev("vector", lambda e, ov=ov: e.tensor_add(ov, ov, x2_tm[:]),
                       wait=[t_ex[1]])
            g4["out"][par] = dma("sync", out_t[b, t0 + s * 128:t0 + (s + 1) * 128, :], ov,
                                 sem=qO, wait=[t_fin])
            tok_out.append(g4["out"][par])
        g4["accrd"] = t_ta
        g4["x2rd"] = t_txx

    qall = [sems[k] for k in sems]
    p.emit("sync", None, wait=[(q, q.n) for q in qall if q.n > 0 and q.h.name.startswith("q")])

    nums = sorted(s.h.num for s in sems.values())
    assert nums == list(range(nums[0], nums[0] + len(nums))), nums
    srange = range(nums[0], nums[-1] + 1)
    if warmup:
        qD = nc.alloc_semaphore("qD")
        for _wd in range(warmup):
            nc.gpsimd.dma_start(out=acc_sp[1, _wd % KD, :, :],
                                in_=w1[_wd % NE, 128:256, 0:S]).then_inc(qD, 16)
        nc.gpsimd.wait_ge(qD, warmup * 16)
    nc.gpsimd.dma_reset(srange)
    nc.gpsimd.sem_clear(srange)
    nc.all_engine_barrier()
    with nc.Block() as block:
        p.replay(block)
    return nc


def make_aux():
    aux = {}
    aux["eye_bf"] = np.eye(128, dtype=BFNP)
    aux["eye_f"] = np.eye(128, dtype=np.float32)
    aux["triu_f"] = np.triu(np.ones((128, 128), np.float32))
    aux["ones_col_bf"] = np.ones((128, 1), BFNP)
    aux["ones_row_f"] = np.ones((1, 128), np.float32)
    selh = np.zeros((HEADS, HEADS * 128), np.float32)
    for h in range(HEADS):
        selh[h, h * 128:(h + 1) * 128] = 1.0
    aux["selh_f"] = selh
    selc = np.zeros((HEADS, 4 * 128), np.float32)
    for j in range(4):
        selc[2 * j, j * 128:j * 128 + 64] = 1.0
        selc[2 * j + 1, j * 128 + 64:j * 128 + 128] = 1.0
    aux["selc_f"] = selc
    pair16 = np.zeros((16, HEADS), np.float32)
    for h in range(HEADS):
        pair16[2 * h, h] = 1.0
        pair16[2 * h + 1, h] = 1.0
    aux["pair16_f"] = pair16
    dbl = np.zeros((HEADS, 16), np.float32)
    for k in range(HEADS):
        dbl[k, 2 * k] = 1.0
        dbl[k, 2 * k + 1] = 1.0
    aux["dblrow_f"] = dbl
    cvals = [-30.0, 1.0, -1.0, -0.5, 0.5, 1.0 / NORM_GN, 1.0 / DIM, 1.0 / S, EPS, 30.0]
    aux["consts_in"] = np.tile(np.array(cvals, np.float32), (128, 1))
    aux["zf_in"] = np.zeros((128, BC * 4 * HD), np.float32)
    aux["zb_in"] = np.zeros((128, BC * 4 * HD), BFNP)
    return aux


def make_in_maps(inputs, n_cores=8):
    x = np.ascontiguousarray(inputs["L0_attn_input_tensor"], dtype=np.float32)
    aux = make_aux()
    shared = dict(
        Wx=np.ascontiguousarray(np.asarray(inputs["L0_attn_Wx"]).T),
        Wz=np.ascontiguousarray(np.asarray(inputs["L0_attn_Wz"]).T),
        convw=np.ascontiguousarray(np.asarray(inputs["L0_attn_conv_w"]).reshape(KI, 128, KER)
                                   .transpose(1, 0, 2).reshape(128, KI * KER)),
        convb=np.ascontiguousarray(np.asarray(inputs["L0_attn_conv_b"]).reshape(KI, 128).T),
        Wb=np.ascontiguousarray(np.asarray(inputs["L0_attn_Wb"]).T),
        Wc=np.ascontiguousarray(np.asarray(inputs["L0_attn_Wc"]).T),
        Wdt=np.ascontiguousarray(np.asarray(inputs["L0_attn_Wdt"]).T),
        bdt=np.ascontiguousarray(np.asarray(inputs["L0_attn_b_dt"]).reshape(HEADS, 1)),
        loga=np.ascontiguousarray(np.asarray(inputs["L0_attn_log_a"]).reshape(HEADS, 1)),
        dpar=np.ascontiguousarray(np.asarray(inputs["L0_attn_d_param"]).reshape(HEADS, 1)),
        gnw=np.ascontiguousarray(np.asarray(inputs["L0_attn_gn_w"]).reshape(KI, 128).T),
        gnb=np.ascontiguousarray(np.asarray(inputs["L0_attn_gn_b"]).reshape(KI, 128).T),
        Wout=np.ascontiguousarray(np.asarray(inputs["L0_attn_Wout"]).T),
        gate_in=np.ascontiguousarray(np.asarray(inputs["L0_ffn_gate_w"]).reshape(KD, 128, NE)
                                     .transpose(1, 0, 2).reshape(128, KD * NE)),
        w1=np.ascontiguousarray(inputs["L0_ffn_w1"]),
        b1_in=np.ascontiguousarray(np.asarray(inputs["L0_ffn_b1"]).reshape(NE, KF, 128)
                                   .transpose(2, 0, 1).reshape(128, NE * KF)),
        w2=np.ascontiguousarray(inputs["L0_ffn_w2"]),
        b2t_in=np.ascontiguousarray(np.asarray(inputs["L0_ffn_b2"]).reshape(NE, KD, 128)
                                    .transpose(2, 1, 0).reshape(128, KD * NE)),
        **aux,
    )
    in_maps = []
    for c in range(n_cores):
        m = dict(shared)
        m["x"] = np.ascontiguousarray(x[BC * c:BC * (c + 1)])
        in_maps.append(m)
    return in_maps


def kernel(**inputs):
    nc = build()
    in_maps = make_in_maps(inputs)
    res = run_bass_kernel_spmd(nc, in_maps, core_ids=list(range(8)))
    return np.concatenate([np.asarray(r["out"], np.float32) for r in res.results], axis=0)
